# version 06
# baseline (speedup 1.0000x reference)
import numpy as np
import ml_dtypes
from contextlib import ExitStack
import concourse.bass as bass
import concourse.mybir as mybir
from concourse.bass_utils import run_bass_kernel_spmd

F32 = mybir.dt.float32
BF16 = mybir.dt.bfloat16
AF = mybir.ActivationFunctionType
ALU = mybir.AluOpType
AX = mybir.AxisListType
BF = ml_dtypes.bfloat16

D = 1024
T = 2048
NCORE = 8
BPC = 2
NT = BPC * T
NTB = NT // 128
DFF0 = 4096
NH = 16
HD = 64
RAILS, RAIL_W, RMULT, RHEADS = 64, 16, 4, 4
DFF1 = 16384
LN_EPS = 1e-5
QEPS = 1e-5


class Sem:
    __slots__ = ("h", "n")
    def __init__(self, h):
        self.h = h
        self.n = 0


class B:
    ENGS = ("y", "g", "v", "s", "t")

    def __init__(self, nc, ctx, sem_offset=0):
        self.nc = nc
        self.L = {k: [] for k in self.ENGS}
        self.sems = {}
        for i in range(sem_offset):
            nc.alloc_semaphore(f"skip{i}")
        for name in ("y", "gd", "gc", "v", "s", "t"):
            self.sems[name] = Sem(ctx.enter_context(nc.semaphore("sem_" + name)))
        self.flush_src = None
        self.flush_dst = None

    def _op(self, eng_key, fn, w, inc_name, inc_by):
        ev = None
        if inc_by:
            sem = self.sems[inc_name]
            sem.n += inc_by
            ev = (inc_name, sem.n)
            tgt = (sem.h, sem.n)
        waits = [(self.sems[n].h, v) for (n, v) in w]

        def run(eng):
            for (h, v) in waits:
                eng.wait_ge(h, v)
            ins = fn(eng)
            if inc_by:
                ins.then_inc(tgt[0], inc_by)
        self.L[eng_key].append(run)
        return ev

    def t(self, fn, w=(), inc=0):
        return self._op("t", fn, w, "t", inc)

    def v(self, fn, w=(), inc=0):
        return self._op("v", fn, w, "v", inc)

    def s(self, fn, w=(), inc=0):
        return self._op("s", fn, w, "s", inc)

    def g(self, fn, w=(), inc=0):
        return self._op("g", fn, w, "gc", inc)

    def dma(self, q, out, in_, w=(), transpose=False, slow=False):
        sem_name = "y" if q == "y" else "gd"
        kw = {}
        if transpose:
            kw["transpose"] = True
        if slow:
            kw["allow_slow_non_contiguous"] = True

        def fn(eng):
            return eng.dma_start(out=out, in_=in_, **kw)
        return self._op(q if q == "y" else "g", fn, w, sem_name, 16)

    def wait(self, eng_key, w):
        waits = [(self.sems[n].h, v) for (n, v) in w]

        def run(eng):
            for (h, v) in waits:
                eng.wait_ge(h, v)
        self.L[eng_key].append(run)

    def bar(self, delay_nops=0):
        if self.flush_src is not None:
            self.dma("y", self.flush_dst, self.flush_src, transpose=True)
        wl = [(n, self.sems[n].n) for n in ("y", "gd", "gc", "v", "s", "t")
              if self.sems[n].n > 0]
        for k in self.ENGS:
            self.wait(k, wl)
            if delay_nops and (DELAY_ENGS is None or k in DELAY_ENGS):
                def run(eng, n=delay_nops):
                    for _ in range(n):
                        eng.nop(cycle_cnt=4000)
                self.L[k].append(run)

    def finalize(self, blk):
        for key, attr in (("y", "sync"), ("g", "gpsimd"), ("v", "vector"),
                          ("s", "scalar"), ("t", "tensor")):
            ops = self.L[key]

            def make(ops):
                def run(eng):
                    for f in ops:
                        f(eng)
                return run
            getattr(blk, attr)(make(ops))


def sbuf(nc, ctx, name, shape, dtype):
    return ctx.enter_context(nc.sbuf_tensor(name, shape, dtype))[:]


def psum(nc, ctx, name, shape, dtype=F32):
    return ctx.enter_context(nc.psum_tensor(name, shape, dtype))[:]


class LNHelper:
    def __init__(self, b, expln=False):
        self.b = b
        self.expln = expln

    def stats(self, x_ap, sq_scratch, stat, col0, width, w=()):
        b = self.b
        c = lambda i: stat[:, col0 + i:col0 + i + 1]
        e = b.v(lambda ve: ve.tensor_tensor(sq_scratch, x_ap, x_ap, op=ALU.mult), w=w, inc=1)
        e = b.v(lambda ve: ve.reduce_sum(c(0), sq_scratch, axis=AX.X), w=[e], inc=1)
        e2 = b.v(lambda ve: ve.reduce_sum(c(1), x_ap, axis=AX.X), w=[e], inc=1)
        e = b.v(lambda ve: ve.tensor_scalar(c(2), c(1), 1.0 / width, None, op0=ALU.mult), w=[e2], inc=1)
        e = b.v(lambda ve: ve.tensor_scalar(c(3), c(0), 1.0 / width, None, op0=ALU.mult), w=[e], inc=1)
        e = b.v(lambda ve: ve.tensor_tensor(c(4), c(2), c(2), op=ALU.mult), w=[e], inc=1)
        e = b.v(lambda ve: ve.tensor_tensor(c(3), c(3), c(4), op=ALU.subtract), w=[e], inc=1)
        e = b.v(lambda ve: ve.tensor_scalar(c(3), c(3), LN_EPS, None, op0=ALU.add), w=[e], inc=1)
        if self.expln:
            e = b.s(lambda sc: sc.activation(c(4), c(3), AF.Ln), w=[e], inc=1)
            e = b.s(lambda sc: sc.activation(c(6), c(4), AF.Exp, scale=-0.5), w=[e], inc=1)
        else:
            e = b.s(lambda sc: sc.activation(c(4), c(3), AF.Sqrt), w=[e], inc=1)
            e = b.v(lambda ve: ve.reciprocal(c(6), c(4)), w=[e], inc=1)
        e = b.v(lambda ve: ve.tensor_tensor(c(5), c(2), c(6), op=ALU.mult), w=[e], inc=1)
        e = b.v(lambda ve: ve.tensor_scalar(c(5), c(5), -1.0, None, op0=ALU.mult), w=[e], inc=1)
        return (c(6), c(5), e)


def stage_inputs(inputs):
    asc = np.ascontiguousarray
    x = np.asarray(inputs["L0_attn_x"], dtype=np.float32)

    shared = {}
    shared["w0qT"] = asc(np.asarray(inputs["L0_attn_Wq"]).T)
    shared["w0vT"] = asc(np.asarray(inputs["L0_attn_Wv"]).T)
    shared["w0gT"] = asc(np.asarray(inputs["L0_attn_Wg"]).T)
    shared["w0oT"] = asc(np.asarray(inputs["L0_attn_Wo"]).T)
    shared["g0full"] = asc(np.broadcast_to(np.asarray(inputs["L0_attn_ln_gamma"])[None, :], (128, D)).astype(np.float32))
    shared["b0full"] = asc(np.broadcast_to(np.asarray(inputs["L0_attn_ln_beta"])[None, :], (128, D)).astype(np.float32))
    shared["decay_row"] = asc(np.asarray(inputs["L0_attn_decay"], dtype=np.float32)[None, :])
    shared["decay_col"] = asc(np.asarray(inputs["L0_attn_decay"], dtype=np.float32)[:, None])

    shared["w1T"] = asc(np.asarray(inputs["L0_ffn_W1"]).T)
    shared["b1t"] = asc(np.asarray(inputs["L0_ffn_b1"], dtype=np.float32).reshape(32, 128).T)
    shared["w2T"] = asc(np.asarray(inputs["L0_ffn_W2"]).T)
    shared["b2r"] = asc(np.asarray(inputs["L0_ffn_b2"], dtype=np.float32)[None, :])

    shared["wqT1"] = asc(np.asarray(inputs["L1_attn_Wq"]).T)
    shared["wkT1"] = asc(np.asarray(inputs["L1_attn_Wk"]).T)
    shared["wvT1"] = asc(np.asarray(inputs["L1_attn_Wv"]).T)
    shared["woT1"] = asc(np.asarray(inputs["L1_attn_Wo"]).T)
    shared["bq1t"] = asc(np.asarray(inputs["L1_attn_bq"], dtype=np.float32).reshape(8, 128).T)
    shared["bk1t"] = asc(np.asarray(inputs["L1_attn_bk"], dtype=np.float32).reshape(8, 128).T)
    shared["bv1r"] = asc(np.asarray(inputs["L1_attn_bv"], dtype=np.float32)[None, :])
    shared["bo1r"] = asc(np.asarray(inputs["L1_attn_bo"], dtype=np.float32)[None, :])
    shared["ascol"] = asc(np.broadcast_to(np.asarray(inputs["L1_attn_attn_scale"], dtype=np.float32).reshape(1, 1), (128, 1)))

    shared["railmJ"] = asc(np.asarray(inputs["L1_ffn_rail_maker"]).transpose(3, 0, 1, 2).reshape(16, 4096))
    shared["railuJ"] = asc(np.asarray(inputs["L1_ffn_rail_up_proj"]).transpose(3, 0, 1, 2).reshape(16, 16384))
    shared["wcT"] = asc(np.asarray(inputs["L1_ffn_w_cproj"]).T)

    shared["c_ones"] = np.ones((1, 512), dtype=BF)
    shared["c_ident"] = asc(np.eye(128, dtype=np.float32).astype(BF))
    qi = np.arange(128)[:, None]
    ki = np.arange(128)[None, :]
    allowed = ((ki // 64) > (qi // 64)) | (((ki // 64) == (qi // 64)) & (ki <= qi))
    shared["c_mask"] = asc(allowed.astype(np.float32).astype(BF))
    shared["c_uptri"] = asc((np.arange(128)[None, :] >= np.arange(128)[:, None]).astype(np.float32))

    in_maps = []
    for c in range(NCORE):
        m = dict(shared)
        m["x_in"] = asc(x[c * BPC:(c + 1) * BPC].reshape(NT, D))
        in_maps.append(m)
    return in_maps


def emit_quant(b, nc, ctx, io, dst_w1q, dst_w2q, ones_col_f32, ones_row_f32):
    wtile = [sbuf(nc, ctx, f"qt_w{i}", [128, 4096], F32) for i in range(2)]
    sgn = sbuf(nc, ctx, "qt_sgn", [128, 4096], F32)
    m01 = sbuf(nc, ctx, "qt_m01", [128, 4096], F32)
    pp = sbuf(nc, ctx, "qt_pp", [128, 48], F32)
    acc = sbuf(nc, ctx, "qt_acc", [128, 4], F32)
    gam = sbuf(nc, ctx, "qt_gam", [1, 4], F32)
    taucol = sbuf(nc, ctx, "qt_tau", [128, 4], F32)
    ps_g = psum(nc, ctx, "qt_psg", [1, 2], F32)
    ps_b = psum(nc, ctx, "qt_psb", [128, 2], F32)

    specs = [("w1", io["w1T"], 8, 4096, dst_w1q, 0),
             ("w2", io["w2T"], 32, 1024, dst_w2q, 1)]
    col = 0
    tile_cols = {}
    lin = []
    for name, src, ntile, width, dst, gi in specs:
        for i in range(ntile):
            par = len(lin) % 2
            wv = [lin[-2]] if len(lin) >= 2 else []
            evl = b.dma("y", wtile[par][:, 0:width], src[i * 128:(i + 1) * 128, :], w=wv)
            ev = b.v(lambda ve, p=par, wd=width, cc=col: ve.reduce_sum(
                pp[:, cc:cc + 1], wtile[p][:, 0:wd], axis=AX.X,
                apply_absolute_value=True), w=[evl], inc=1)
            tile_cols[(name, i)] = ev
            lin.append(ev)
            col += 1
    ev = b.v(lambda ve: ve.reduce_sum(acc[:, 0:1], pp[:, 0:8], axis=AX.X), inc=1)
    ev2 = b.v(lambda ve: ve.reduce_sum(acc[:, 1:2], pp[:, 8:40], axis=AX.X), w=[ev], inc=1)
    evm = b.t(lambda te: te.matmul(ps_g, ones_col_f32, acc[:, 0:2], start=True, stop=True),
              w=[ev2], inc=1)
    ev3 = b.s(lambda sc: sc.copy(gam[:, 0:2], ps_g), w=[evm], inc=1)
    ev4 = b.s(lambda sc: sc.activation(gam[:, 2:4], gam[:, 0:2], AF.Copy,
                                       bias=QEPS, scale=1.0 / (4096 * 1024)), w=[ev3], inc=1)
    evb = b.t(lambda te: te.matmul(ps_b, ones_row_f32, gam[0:1, 2:4], start=True, stop=True),
              w=[ev4], inc=1)
    evt0 = b.s(lambda sc: sc.activation(taucol[:, 0:2], ps_b, AF.Copy, bias=0.0, scale=0.5), w=[evb], inc=1)
    evt = b.s(lambda sc: sc.activation(taucol[:, 2:4], taucol[:, 0:2], AF.Copy, bias=0.0, scale=-1.0), w=[evt0], inc=1)

    prev_q = {}
    last_q = None
    lin2 = []
    for name, src, ntile, width, dst, gi in specs:
        for i in range(ntile):
            par = len(lin2) % 2
            wv = [lin2[-2]] if len(lin2) >= 2 else [lin[-2 + len(lin2)]]
            evl = b.dma("y", wtile[par][:, 0:width], src[i * 128:(i + 1) * 128, :], w=wv + [evt])
            wv_s = [evl] + ([last_q] if last_q is not None else [])
            evs = b.v(lambda ve, p=par, wd=width, g=gi: ve.tensor_scalar(
                sgn[:, 0:wd], wtile[p][:, 0:wd], taucol[:, g:g + 1], None,
                op0=ALU.is_gt), w=wv_s, inc=1)
            evm1 = b.v(lambda ve, p=par, wd=width, g=gi: ve.tensor_scalar(
                m01[:, 0:wd], wtile[p][:, 0:wd], taucol[:, 2 + g:3 + g], None,
                op0=ALU.is_lt), w=[evl], inc=1)
            if name == "w1":
                dslice = dst[:, i * 4096:(i + 1) * 4096]
            else:
                dslice = dst[:, i * 1024:(i + 1) * 1024]
            evq = b.v(lambda ve, ds=dslice, wd=width: ve.tensor_tensor(
                ds, sgn[:, 0:wd], m01[:, 0:wd], op=ALU.subtract), w=[evm1], inc=1)
            prev_q[(name, i)] = evq
            last_q = evq
            lin2.append(evq)
    return prev_q[("w2", 31)]


def emit_p2(b, nc, ctx, io):
    ln = LNHelper(b)
    ones_col = sbuf(nc, ctx, "p2_onescol", [128, 1], F32)
    ones_row = sbuf(nc, ctx, "p2_onesrow", [1, 128], F32)
    ev_oc = b.v(lambda ve: ve.memset(ones_col, 1.0), inc=1)
    ev_or = b.v(lambda ve: ve.memset(ones_row, 1.0), inc=1)
    b.wait("t", [ev_oc, ev_or])

    w1q = sbuf(nc, ctx, "p2_w1q", [128, 8 * 4096], BF16)
    w2q = sbuf(nc, ctx, "p2_w2q", [128, 32 * 1024], BF16)
    with ExitStack() as qctx:
        ev_q = emit_quant(b, nc, qctx, io, w1q, w2q, ones_col, ones_row)
    if "dbg_w1q" in io:
        ev_q = b.dma("y", io["dbg_w1q"], w1q[:, 0:4096], w=[ev_q])
        ev_q = b.dma("y", io["dbg_w2q"], w2q[:, 0:4096], w=[ev_q])

    b1t = sbuf(nc, ctx, "p2_b1t", [128, 32], F32)
    b2rb = sbuf(nc, ctx, "p2_b2rb", [1, 1024], BF16)
    b2rf = sbuf(nc, ctx, "p2_b2rf", [1, 1024], F32)
    onesb = sbuf(nc, ctx, "p2_onesb", [1, 128], BF16)
    evb1 = b.dma("y", b1t, io["b1t"])
    evb2 = b.dma("y", b2rf, io["b2r"])
    evb2b = b.s(lambda sc: sc.copy(b2rb, b2rf), w=[evb2], inc=1)
    evob = b.v(lambda ve: ve.memset(onesb, 1.0), inc=1)

    NG = 8
    GT = 512
    xnt0 = sbuf(nc, ctx, "p2_xnt0", [128, 8 * GT], BF16)
    xnt = [xnt0, xnt0]
    xt0 = sbuf(nc, ctx, "p2_xt0", [128, 4 * 1024], F32)
    xt = [xt0, xt0]
    ht = sbuf(nc, ctx, "p2_ht", [128, 32 * GT], BF16)
    dbuf = sbuf(nc, ctx, "p2_dbuf", [128, 1024], F32)
    stat = sbuf(nc, ctx, "p2_stat", [128, 16], F32)
    x2t = sbuf(nc, ctx, "p2_x2t", [128, 1024], F32)
    xn2t = sbuf(nc, ctx, "p2_xn2t", [128, 1024], F32)
    xn2b = sbuf(nc, ctx, "p2_xn2b", [128, 1024], BF16)

    ps_h = [psum(nc, ctx, f"p2_psh{i}", [128, GT], F32) for i in range(2)]
    ps_o = [psum(nc, ctx, f"p2_pso{i}", [128, 512], F32) for i in range(2)]

    load_ev = {}

    def issue_loads(g, w, wx=()):
        par = g % 2
        ev = None
        for fic in range(8):
            ev = b.dma("y", xnt[par][:, fic * GT:(fic + 1) * GT],
                       io["xn_in"][g * GT:(g + 1) * GT, fic * 128:(fic + 1) * 128],
                       w=w, transpose=True)
        evx = None
        for tb in range(4):
            evx = b.dma("y", xt[par][:, tb * 1024:(tb + 1) * 1024],
                        io["x_in"][g * GT + tb * 128:g * GT + (tb + 1) * 128, :],
                        w=(wx if tb == 0 else ()))
        load_ev[g] = (ev, evx)

    issue_loads(0, [ev_q], wx=[ev_q])

    prev_sx = None
    prev_sxn = None
    consumed_xnt_prev = None
    group_done = {}

    for g in range(NG):
        par = g % 2
        ev_xnt, ev_xt = load_ev[g]
        h_done = []
        for foc in range(32):
            pb = foc % 2
            wv = [ev_xnt]
            if foc >= 2:
                wv.append(h_done[foc - 2])
            ev_mm = None
            for fic in range(8):
                ev_mm = b.t(lambda te, pb=pb, fic=fic, foc=foc, par=par: te.matmul(
                    ps_h[pb], w1q[:, fic * 4096 + foc * 128: fic * 4096 + (foc + 1) * 128],
                    xnt[par][:, fic * GT:(fic + 1) * GT],
                    start=(fic == 0), stop=(fic == 7)),
                    w=wv if fic == 0 else (), inc=(1 if fic == 7 else 0))
            wv_s = [ev_mm, evb1]
            ev_ev = b.s(lambda sc, pb=pb, foc=foc: sc.activation(
                ht[:, foc * GT:(foc + 1) * GT], ps_h[pb], AF.Relu,
                bias=b1t[:, foc:foc + 1], scale=1.0), w=wv_s, inc=1)
            h_done.append(ev_ev)
        if g == 0 and "dbg_xnt" in io:
            b.dma("y", io["dbg_xnt"], xnt[0][:, 0:4096], w=[ev_xnt])
            h_done[-1] = b.dma("y", io["dbg_ht"], ht[:, 0:1024], w=[h_done[0], h_done[1]])
        consumed_xnt = h_done[-1]

        prev_evac = []
        ev_x2 = None
        for tsub in range(4):
            ev_os = []
            for dh in range(2):
                wv = [h_done[-1], evb2b, evob] + prev_evac
                ev_o = None
                for foc in range(32):
                    b.t(lambda te, dh=dh, foc=foc, tsub=tsub: te.matmul(
                        ps_o[dh], ht[:, foc * GT + tsub * 128: foc * GT + tsub * 128 + 128],
                        w2q[:, foc * 1024 + dh * 512: foc * 1024 + (dh + 1) * 512],
                        start=(foc == 0), stop=False),
                        w=wv if foc == 0 else ())
                ev_o = b.t(lambda te, dh=dh: te.matmul(
                    ps_o[dh], onesb, b2rb[0:1, dh * 512:(dh + 1) * 512],
                    start=False, stop=True), inc=1)
                ev_os.append(ev_o)
            wv_d = [ev_os[1]]
            if prev_sx is not None:
                wv_d.append(prev_sx)
            if prev_sxn is not None:
                wv_d.append(prev_sxn)
            ev_d0 = b.v(lambda ve: ve.tensor_copy(dbuf[:, 0:512], ps_o[0]), w=wv_d, inc=1)
            ev_d1 = b.v(lambda ve: ve.tensor_copy(dbuf[:, 512:1024], ps_o[1]), inc=1)
            prev_evac = [ev_d0, ev_d1]
            if g == 0 and tsub == 0 and "dbg_dbuf" in io:
                prev_evac.append(b.dma("y", io["dbg_dbuf"], dbuf, w=[ev_d1]))
            a_c, b_c, ev_st = ln.stats(dbuf, x2t, stat, 0, D, w=[ev_d1])
            if g == 0 and tsub == 0 and "dbg_stat" in io:
                b.dma("y", io["dbg_stat"], stat, w=[ev_st])
            ev_ap = b.v(lambda ve: ve.tensor_scalar(
                x2t, dbuf, a_c, b_c, op0=ALU.mult, op1=ALU.add), w=[ev_st])
            ev_x2 = b.v(lambda ve, par=par, tsub=tsub: ve.tensor_tensor(
                x2t, x2t, xt[par][:, tsub * 1024:(tsub + 1) * 1024], op=ALU.add),
                w=[ev_xt], inc=1)
            prev_sx = b.dma("y", io["x_out"][g * GT + tsub * 128: g * GT + (tsub + 1) * 128, :],
                            x2t, w=[ev_x2])
            a2, b2_, ev_st2 = ln.stats(x2t, xn2t, stat, 8, D, w=[ev_x2])
            ev_ap2 = b.v(lambda ve: ve.tensor_scalar(
                xn2t, x2t, a2, b2_, op0=ALU.mult, op1=ALU.add), w=[ev_st2], inc=1)
            wv_cb2 = [ev_ap2] + ([prev_sxn] if prev_sxn is not None else [])
            ev_cb2 = b.v(lambda ve: ve.tensor_copy(xn2b, xn2t), w=wv_cb2, inc=1)
            prev_sxn = b.dma("y", io["xn_out"][g * GT + tsub * 128: g * GT + (tsub + 1) * 128, :],
                             xn2b, w=[ev_cb2])
        group_done[g] = (consumed_xnt, ev_x2)
        if g + 1 < NG:
            issue_loads(g + 1, [consumed_xnt], wx=[ev_x2])

    b.wait("y", [("y", b.sems["y"].n)])
    b.wait("g", [("gd", b.sems["gd"].n)])


def emit_p1(b, nc, ctx, io):
    ln = LNHelper(b)
    xn0buf = io["xn0buf"]
    y2buf = io["y2buf"]
    r_dram = io["r_dram"]
    lamp_df = io["lamp_df"]
    lamp_db = io["lamp_db"]
    l128_d = io["l128_d"]

    wq_s = sbuf(nc, ctx, "p1_wq", [128, 8 * 1024], BF16)
    wv_s = sbuf(nc, ctx, "p1_wv", [128, 8 * 1024], BF16)
    wg_s = sbuf(nc, ctx, "p1_wg", [128, 8 * 1024], BF16)
    wo_s = sbuf(nc, ctx, "p1_wo", [128, 8 * 1024], BF16)
    g0f = sbuf(nc, ctx, "p1_g0", [128, 1024], F32)
    b0f = sbuf(nc, ctx, "p1_b0", [128, 1024], F32)
    lt_s = sbuf(nc, ctx, "p1_lt", [128, 16 * 128], BF16)
    lam_s = sbuf(nc, ctx, "p1_lam", [128, 16], BF16)
    lamrow = sbuf(nc, ctx, "p1_lamrow", [1, 2048], BF16)
    l128row = sbuf(nc, ctx, "p1_l128row", [1, 2048], F32)
    carry = sbuf(nc, ctx, "p1_carry", [1, 2048], F32)
    cin_bf = sbuf(nc, ctx, "p1_cin", [1, 4096], BF16)
    t1row = sbuf(nc, ctx, "p1_t1row", [1, 2048], F32)
    rrow = sbuf(nc, ctx, "p1_rrow", [1, 4096], F32)
    r_sb = sbuf(nc, ctx, "p1_rsb", [16, 512], F32)
    onesr = sbuf(nc, ctx, "p1_onesr", [1, 128], F32)
    stat = sbuf(nc, ctx, "p1_stat", [128, 16], F32)
    sq = sbuf(nc, ctx, "p1_sq", [128, 1024], F32)

    ev_w = None
    for dst, key in ((wq_s, "w0qT"), (wv_s, "w0vT"), (wg_s, "w0gT"), (wo_s, "w0oT")):
        for fic in range(8):
            ev_w = b.dma("g", dst[:, fic * 1024:(fic + 1) * 1024],
                         io[key][fic * 128:(fic + 1) * 128, :])
    ev_g0 = b.dma("y", g0f, io["g0full"])
    ev_b0 = b.dma("y", b0f, io["b0full"])

    sctx = ExitStack()
    dec_c = sbuf(nc, sctx, "p1_dec", [16, 1], F32)
    lncol = sbuf(nc, sctx, "p1_lncol", [16, 1], F32)
    lnrow = sbuf(nc, sctx, "p1_lnrow", [1, 16], F32)
    lnb = sbuf(nc, sctx, "p1_lnb", [128, 16], F32)
    dmat = sbuf(nc, sctx, "p1_dmat", [128, 128], F32)
    drelu = sbuf(nc, sctx, "p1_drelu", [128, 128], F32)
    uptri = sbuf(nc, sctx, "p1_uptri", [128, 128], F32)
    ltf = sbuf(nc, sctx, "p1_ltf", [128, 128], F32)
    jtab = sbuf(nc, sctx, "p1_jtab", [16, 130], F32)
    lamp = sbuf(nc, sctx, "p1_lamp", [16, 130], F32)
    i127 = sbuf(nc, sctx, "p1_i127", [128, 1], F32)
    l128r16 = sbuf(nc, sctx, "p1_l128r16", [1, 16], F32)
    l128t = sbuf(nc, sctx, "p1_l128t", [16, 128], F32)
    ps_bc = psum(nc, sctx, "p1_psbc", [128, 16], F32)
    ps_b2 = psum(nc, sctx, "p1_psb2", [16, 128], F32)

    e_or = b.v(lambda ve: ve.memset(onesr, 1.0), inc=1)
    e_cz = b.v(lambda ve: ve.memset(carry, 0.0), inc=1)
    e_dc = b.dma("y", dec_c, io["decay_col"])
    e_dr = b.dma("y", lnrow, io["decay_row"])
    e_ut = b.dma("y", uptri, io["c_uptri"])
    e1 = b.s(lambda sc: sc.activation(lncol, dec_c, AF.Ln), w=[e_dc], inc=1)
    e2 = b.s(lambda sc: sc.activation(lnrow, lnrow, AF.Ln), w=[e_dr], inc=1)
    e3 = b.t(lambda te: te.matmul(ps_bc, onesr, lnrow, start=True, stop=True),
             w=[e_or, e2], inc=1)
    e4 = b.s(lambda sc: sc.copy(lnb, ps_bc), w=[e3], inc=1)
    e5 = b.g(lambda gp: gp.iota(dmat, pattern=[[1, 128]], base=0, channel_multiplier=-1,
                                allow_small_or_imprecise_dtypes=True), inc=1)
    e6 = b.v(lambda ve: ve.tensor_scalar(drelu, dmat, 0.0, None, op0=ALU.max), w=[e5], inc=1)
    e_lt = None
    for h in range(16):
        ea = b.s(lambda sc, h=h: sc.activation(ltf, drelu, AF.Exp, scale=lnb[:, h:h + 1]),
                 w=[e6, e4] + ([e_lt] if e_lt else []), inc=1)
        e_lt = b.v(lambda ve, h=h: ve.tensor_tensor(
            lt_s[:, h * 128:(h + 1) * 128], ltf, uptri, op=ALU.mult), w=[ea, e_ut], inc=1)
    e7 = b.g(lambda gp: gp.iota(i127, pattern=[[1, 1]], base=127, channel_multiplier=-1,
                                allow_small_or_imprecise_dtypes=True), inc=1)
    e_lam = None
    for h in range(16):
        e_lam = b.s(lambda sc, h=h: sc.activation(
            lam_s[:, h:h + 1], i127, AF.Exp, scale=lnb[:, h:h + 1]), w=[e7, e4], inc=1)
    e8 = b.g(lambda gp: gp.iota(jtab, pattern=[[1, 130]], base=0, channel_multiplier=0,
                                allow_small_or_imprecise_dtypes=True), inc=1)
    e9 = b.s(lambda sc: sc.activation(lamp, jtab, AF.Exp, scale=lncol), w=[e8, e1], inc=1)
    lampb = sbuf(nc, sctx, "p1_lampb", [16, 130], BF16)
    e10f = b.dma("y", lamp_df, lamp, w=[e9])
    e10bc = b.s(lambda sc: sc.copy(lampb, lamp), w=[e9], inc=1)
    e10b = b.dma("y", lamp_db, lampb, w=[e10bc])
    e11 = b.dma("y", lamrow, lamp_db[0:16, 1:129], w=[e10b])
    e12 = b.dma("y", l128r16, lamp_df[0:16, 128:129], w=[e10f], slow=True)
    e13 = b.t(lambda te: te.matmul(ps_b2, l128r16, onesr, start=True, stop=True),
              w=[e12, e_or], inc=1)
    e14 = b.s(lambda sc: sc.copy(l128t, ps_b2), w=[e13], inc=1)
    e15 = b.dma("y", l128_d, l128t, w=[e14])
    e16 = b.dma("y", l128row, l128_d[0:16, :], w=[e15])
    setup_done = [e_lt, e_lam, e10b, e16, e11]
    sctx.close()

    actx = ExitStack()
    xa = [sbuf(nc, actx, f"p1a_x{i}", [128, 1024], F32) for i in range(2)]
    xna = [sbuf(nc, actx, f"p1a_xn{i}", [128, 1024], F32) for i in range(2)]
    xnab = [sbuf(nc, actx, f"p1a_xnb{i}", [128, 1024], BF16) for i in range(2)]
    p1a_done = []
    prev_store = [None, None]
    for tt in range(NTB):
        par = tt % 2
        wv = [p1a_done[tt - 2][1]] if tt >= 2 else list(setup_done)
        evl = b.dma("y", xa[par], io["x_in"][tt * 128:(tt + 1) * 128, :], w=wv)
        a_c, b_c, ev_st = ln.stats(xa[par], xna[par], stat, 0, D,
                                   w=[evl] + ([prev_store[par]] if prev_store[par] else []))
        ev_ap = b.v(lambda ve, par=par: ve.tensor_scalar(
            xna[par], xa[par], a_c, b_c, op0=ALU.mult, op1=ALU.add), w=[ev_st], inc=1)
        ev_cb = b.v(lambda ve, par=par: ve.tensor_copy(xnab[par], xna[par]), w=[ev_ap], inc=1)
        ev_s = b.dma("y", xn0buf[tt * 128:(tt + 1) * 128, :], xnab[par], w=[ev_cb])
        prev_store[par] = ev_s
        p1a_done.append((ev_ap, ev_s))

    p1a_all = ("gd", b.sems["gd"].n)
    p1a_all_v = ("v", b.sems["v"].n)
    actx.close()

    xnt = sbuf(nc, ctx, "p1_xnt", [128, 8 * 512], BF16)
    xt = sbuf(nc, ctx, "p1_xt", [128, 1024], F32)
    q_s = sbuf(nc, ctx, "p1_qs", [128, 4 * 1024], F32)
    v_s = sbuf(nc, ctx, "p1_vs", [128, 4 * 1024], BF16)
    gate_s = sbuf(nc, ctx, "p1_gate", [128, 4 * 1024], F32)
    y2b = sbuf(nc, ctx, "p1_y2b", [128, 1024], BF16)
    y2t = sbuf(nc, ctx, "p1_y2t", [128, 8 * 128], BF16)
    obuf = sbuf(nc, ctx, "p1_obuf", [128, 1024], F32)
    x1t = sbuf(nc, ctx, "p1_x1t", [128, 1024], F32)
    xn1t = sbuf(nc, ctx, "p1_xn1t", [128, 1024], F32)
    xn1b = sbuf(nc, ctx, "p1_xn1b", [128, 1024], BF16)

    pq = psum(nc, ctx, "p1_pq", [128, 1024], F32)
    pv = psum(nc, ctx, "p1_pv", [128, 1024], F32)
    pg = psum(nc, ctx, "p1_pg", [128, 1024], F32)
    ps_st = psum(nc, ctx, "p1_psst", [128, 256], F32)
    ps_r = psum(nc, ctx, "p1_psr", [16, 512], F32)

    def strided4(t, h, width=1024):
        return t.rearrange("p (blk f) -> p blk f", blk=4)[:, :, h * 64:(h + 1) * 64]

    NS = 8
    prev = {}
    for st in range(NS):
        rows = [st * 256, st * 256 + 128, 2048 + st * 256, 2048 + st * 256 + 128]
        wv_xn = [prev["xnt_free"]] if "xnt_free" in prev else [ev_w, p1a_all, p1a_all_v]
        need = []
        ev_xnt = None
        for seq in range(2):
            r0 = rows[seq * 2]
            for fic in range(8):
                ev_xnt = b.dma("y", xnt[:, fic * 512 + seq * 256: fic * 512 + seq * 256 + 256],
                               xn0buf[r0:r0 + 256, fic * 128:(fic + 1) * 128],
                               w=(wv_xn + need) if (seq == 0 and fic == 0) else (),
                               transpose=True)
        ev_evq = ev_evv = ev_evg = None
        sil_evs = []
        setup_ps_waits = [e4, e14] if st == 0 else []
        for blk in range(4):
            seq, ch = blk // 2, blk % 2
            wv0 = [ev_xnt] + setup_ps_waits
            if "q_free" in prev and blk == 0:
                wv0 += [prev["q_free"], prev["v_free"]]
            if blk > 0:
                wv0 = [ev_evg, ev_evv]
            ev_mm = None
            for fic in range(8):
                lhs = xnt[:, fic * 512 + seq * 256 + ch * 128: fic * 512 + seq * 256 + ch * 128 + 128]
                for wi, (ws, ps) in enumerate(((wq_s, pq), (wv_s, pv), (wg_s, pg))):
                    for half in range(2):
                        ev_mm = b.t(lambda te, lhs=lhs, ws=ws, ps=ps, fic=fic, half=half: te.matmul(
                            ps[:, half * 512:(half + 1) * 512], lhs,
                            ws[:, fic * 1024 + half * 512: fic * 1024 + (half + 1) * 512],
                            start=(fic == 0), stop=(fic == 7)),
                            w=wv0 if (fic == 0 and wi == 0 and half == 0) else (),
                            inc=(1 if (fic == 7 and wi == 2 and half == 1) else 0))
            ev_evq = b.v(lambda ve, blk=blk: ve.tensor_copy(
                q_s[:, blk * 1024:(blk + 1) * 1024], pq), w=[ev_mm], inc=1)
            ev_evv = b.v(lambda ve, blk=blk: ve.tensor_copy(
                v_s[:, blk * 1024:(blk + 1) * 1024], pv), inc=1)
            wv_sil = [ev_mm]
            if blk == 0 and "g_free" in prev:
                wv_sil.append(prev["g_free"])
            ev_evg = b.s(lambda sc, blk=blk: sc.activation(
                gate_s[:, blk * 1024:(blk + 1) * 1024], pg, AF.Silu), w=wv_sil, inc=1)
            sil_evs.append(ev_evg)
        ev_vs_done = ev_evv

        ev_rc = None
        for k in range(8):
            ev_rm = b.t(lambda te, k=k: te.matmul(
                ps_r, lam_s, v_s[:, k * 512:(k + 1) * 512], start=True, stop=True),
                w=[ev_vs_done, e_lam] if k == 0 else ([ev_rc] if ev_rc else ()), inc=1)
            wv_rc = [ev_rm]
            if "rsb_free" in prev:
                wv_rc.append(prev["rsb_free"])
            ev_rc = b.s(lambda sc: sc.copy(r_sb, ps_r), w=wv_rc, inc=1)
            wv_rs = [ev_rc]
            if k == 0 and "rdram_free" in prev:
                wv_rs.append(prev["rdram_free"])
            ev_rst = b.dma("y", r_dram[:, k * 512:(k + 1) * 512], r_sb, w=wv_rs)
            prev["rsb_free"] = ev_rst
        ev_rl = None
        for h in range(16):
            src_v = r_dram.rearrange("h (blk f) -> h blk f", blk=4)[h:h + 1, :, h * 64:(h + 1) * 64]
            dst_v = rrow[0:1, h * 256:(h + 1) * 256].rearrange("o (blk f) -> o blk f", blk=4)
            wv_rl = [ev_rst] if h == 0 else ()
            if h == 0 and "rrow_free" in prev:
                wv_rl = [ev_rst, prev["rrow_free"]]
            ev_rl = b.dma("y", dst_v, src_v, w=wv_rl)
        prev["rdram_free"] = ev_rl
        r4 = rrow.rearrange("o (h blk f) -> o h blk f", h=16, blk=4)
        ci4 = cin_bf.rearrange("o (h blk f) -> o h blk f", h=16, blk=4)
        c2 = carry.rearrange("o (h s f) -> o h s f", h=16, s=2)
        t2 = t1row.rearrange("o (h s f) -> o h s f", h=16, s=2)
        e = b.v(lambda ve: ve.tensor_copy(ci4[:, :, 0::2, :], c2), w=[ev_rl, e11, e16] + ([prev["cin_free"]] if "cin_free" in prev else []), inc=1)
        e = b.v(lambda ve: ve.tensor_tensor(t1row, carry, l128row, op=ALU.mult), w=[e], inc=1)
        e = b.v(lambda ve: ve.tensor_tensor(t2, t2, r4[:, :, 0::2, :], op=ALU.add), w=[e], inc=1)
        e = b.v(lambda ve: ve.tensor_copy(ci4[:, :, 1::2, :], t2), w=[e], inc=1)
        e = b.v(lambda ve: ve.tensor_tensor(carry, t1row, l128row, op=ALU.mult), w=[e], inc=1)
        ev_chain = b.v(lambda ve: ve.tensor_tensor(c2, c2, r4[:, :, 1::2, :], op=ALU.add), w=[e], inc=1)
        prev["rrow_free"] = ev_chain

        ev_y = None
        ev_int = None
        for h in range(16):
            wv0 = [ev_chain] if h == 0 else ([ev_y])
            b.t(lambda te, h=h: te.matmul(
                ps_st, lt_s[:, h * 128:(h + 1) * 128], strided4(v_s, h),
                start=True, stop=False), w=wv0)
            ev_int = b.t(lambda te, h=h: te.matmul(
                ps_st, lamrow[0:1, h * 128:(h + 1) * 128], cin_bf[0:1, h * 256:(h + 1) * 256],
                start=False, stop=True), inc=1)
            ev_y = b.v(lambda ve, h=h: ve.tensor_tensor(
                strided4(q_s, h), ps_st.rearrange("p (blk f) -> p blk f", blk=4),
                strided4(q_s, h), op=ALU.mult), w=[ev_int], inc=1)
        prev["cin_free"] = ev_y
        prev["q_free"] = ev_y
        prev["xnt_free"] = ev_int

        ev_x1_last = None
        for blk in range(4):
            wv_xt = [prev["xt_free"]] if "xt_free" in prev else []
            ev_xt = b.dma("y", xt, io["x_in"][rows[blk]:rows[blk] + 128, :], w=wv_xt)
            prev["xt_free"] = None
            ys = q_s[:, blk * 1024:(blk + 1) * 1024]
            a_c, b_c, ev_st2 = ln.stats(ys, sq, stat, 0, D, w=[ev_y])
            e = b.v(lambda ve, ys=ys: ve.tensor_scalar(ys, ys, a_c, b_c, op0=ALU.mult, op1=ALU.add), w=[ev_st2], inc=1)
            e = b.v(lambda ve, ys=ys: ve.tensor_tensor(ys, ys, g0f, op=ALU.mult), w=[e, ev_g0], inc=1)
            e = b.v(lambda ve, ys=ys: ve.tensor_tensor(ys, ys, b0f, op=ALU.add), w=[e, ev_b0], inc=1)
            wv_y2 = [e, sil_evs[blk]] + ([prev["y2b_free"]] if "y2b_free" in prev else [])
            ev_y2 = b.v(lambda ve, ys=ys, blk=blk: ve.tensor_tensor(
                y2b, ys, gate_s[:, blk * 1024:(blk + 1) * 1024], op=ALU.mult), w=wv_y2, inc=1)
            r0 = rows[blk]
            ev_y2s = b.dma("y", y2buf[r0:r0 + 128, :], y2b, w=[ev_y2])
            prev["y2b_free"] = ev_y2s
            ev_y2t = None
            for fic in range(8):
                ev_y2t = b.dma("y", y2t[:, fic * 128:(fic + 1) * 128],
                               y2buf[r0:r0 + 128, fic * 128:(fic + 1) * 128],
                               w=[ev_y2s, prev.get("y2t_free") or ev_y2s] if fic == 0 else (),
                               transpose=True)
            pq_free = prev.get("pq_free") or ev_evq
            ev_wo = None
            for fic in range(8):
                for half in range(2):
                    ev_wo = b.t(lambda te, fic=fic, half=half: te.matmul(
                        pq[:, half * 512:(half + 1) * 512], y2t[:, fic * 128:(fic + 1) * 128],
                        wo_s[:, fic * 1024 + half * 512: fic * 1024 + (half + 1) * 512],
                        start=(fic == 0), stop=(fic == 7)),
                        w=[ev_y2t, pq_free] if (fic == 0 and half == 0) else (),
                        inc=(1 if (fic == 7 and half == 1) else 0))
            prev["y2t_free"] = ev_wo
            ev_ob = b.v(lambda ve: ve.tensor_copy(obuf, pq), w=[ev_wo], inc=1)
            prev["pq_free"] = ev_ob
            wv_st3 = [ev_ob] + ([prev["x1t_free"]] if "x1t_free" in prev else [])
            a2, b2c, ev_st3 = ln.stats(obuf, x1t, stat, 8, D, w=wv_st3)
            e = b.v(lambda ve: ve.tensor_scalar(x1t, obuf, a2, b2c, op0=ALU.mult, op1=ALU.add), w=[ev_st3], inc=1)
            ev_x1 = b.v(lambda ve: ve.tensor_tensor(
                x1t, x1t, xt, op=ALU.add), w=[e, ev_xt], inc=1)
            ev_sx = b.dma("y", io["x_out"][r0:r0 + 128, :], x1t, w=[ev_x1])
            wv_st4 = [ev_x1] + ([prev["xn1t_free"]] if "xn1t_free" in prev else [])
            a3, b3c, ev_st4 = ln.stats(x1t, xn1t, stat, 0, D, w=wv_st4)
            ev_ap3 = b.v(lambda ve: ve.tensor_scalar(
                xn1t, x1t, a3, b3c, op0=ALU.mult, op1=ALU.add), w=[ev_st4], inc=1)
            wv_cb = [ev_ap3] + ([prev["xn1b_free"]] if "xn1b_free" in prev else [])
            ev_cb3 = b.v(lambda ve: ve.tensor_copy(xn1b, xn1t), w=wv_cb, inc=1)
            ev_sxn = b.dma("y", io["xn_out"][r0:r0 + 128, :], xn1b, w=[ev_cb3])
            prev["xn1b_free"] = ev_sxn
            ev_x1_last = ev_sx
            prev["x1t_free"] = ev_sx
            prev["xn1t_free"] = ev_sxn
            prev["xt_free"] = ev_x1
        prev["q_free"] = prev["pq_free"]
        prev["v_free"] = ev_evv
        prev["g_free"] = ("v", b.sems["v"].n)

    b.wait("y", [("y", b.sems["y"].n)])
    b.wait("g", [("gd", b.sems["gd"].n)])


def emit_p4(b, nc, ctx, io):
    ln = LNHelper(b)
    wbd_d = io["wbd_d"]
    wcb_d = io["wcb_d"]

    ev_wc = None
    for i in range(16):
        ev_wc = b.dma("g", wcb_d[i * 1024:(i + 1) * 1024, :],
                      io["wcT"][i * 1024:(i + 1) * 1024, :])
    with ExitStack() as sctx:
        rm_s = sbuf(nc, sctx, "p4_rm", [16, 4096], BF16)
        ru_s = sbuf(nc, sctx, "p4_ru", [16, 16384], BF16)
        cstage = sbuf(nc, sctx, "p4_cst", [16, 64], BF16)
        zt = sbuf(nc, sctx, "p4_zt", [128, 512], BF16)
        ps_c = [psum(nc, sctx, f"p4_psc{i}", [16, 64], F32) for i in range(2)]
        ev_rm = b.dma("g", rm_s, io["railmJ"])
        ev_ru = b.dma("g", ru_s, io["railuJ"])
        ev_z = b.v(lambda ve: ve.memset(zt, 0.0), inc=1)
        ev_zs = None
        for i in range(32):
            ev_zs = b.dma("y", wbd_d[i * 128:(i + 1) * 128, :], zt, w=[ev_z] if i == 0 else ())
        ev_cp = None
        ev_cs = None
        for h in range(4):
            for r in range(64):
                pi = r % 2
                wv = [ev_rm, ev_ru] if (h == 0 and r == 0) else ([ev_cp] if r >= 2 or h > 0 else [])
                ev_mmc = b.t(lambda te, h=h, r=r, pi=pi: te.matmul(
                    ps_c[pi], rm_s[:, (h * 64 + r) * 16:(h * 64 + r + 1) * 16],
                    ru_s[:, (h * 64 + r) * 64:(h * 64 + r + 1) * 64],
                    start=True, stop=True), w=wv, inc=1)
                wv_c = [ev_mmc] + ([ev_cs] if ev_cs else [ev_zs])
                ev_cp = b.s(lambda sc, pi=pi: sc.copy(cstage, ps_c[pi]), w=wv_c, inc=1)
                g, rl = r // 8, r % 8
                ev_cs = b.dma("y", wbd_d[(h * 8 + g) * 128 + rl * 16:(h * 8 + g) * 128 + rl * 16 + 16,
                                         rl * 64:(rl + 1) * 64], cstage, w=[ev_cp])
        ev_bd_done = ev_cs

    wbd_s = sbuf(nc, ctx, "p4_wbd", [128, 32 * 512], BF16)
    ht = sbuf(nc, ctx, "p4_ht", [128, 128 * 512], BF16)
    xnt = sbuf(nc, ctx, "p4_xnt", [128, 8 * 512], BF16)
    wcs = [sbuf(nc, ctx, f"p4_wcs{i}", [128, 1024], BF16) for i in range(2)]
    xt = sbuf(nc, ctx, "p4_xt", [128, 1024], F32)
    dbuf = sbuf(nc, ctx, "p4_dbuf", [128, 4 * 1024], F32)
    x2t = sbuf(nc, ctx, "p4_x2t", [128, 1024], F32)
    stat = sbuf(nc, ctx, "p4_stat", [128, 16], F32)
    ps_z = [psum(nc, ctx, f"p4_psz{i}", [128, 512], F32) for i in range(2)]
    ps_o = [psum(nc, ctx, f"p4_pso{i}", [128, 512], F32) for i in range(4)]

    ev_bd = None
    for i in range(32):
        ev_bd = b.dma("y", wbd_s[:, i * 512:(i + 1) * 512],
                      wbd_d[i * 128:(i + 1) * 128, :], w=[ev_bd_done] if i == 0 else ())
    if "dbg_wbd" in io:
        b.dma("y", io["dbg_wbd"], wbd_s[:, 0:512], w=[ev_bd])

    NG = 8
    GT = 512
    prev = {}
    for g in range(NG):
        wv_xn = [prev["xnt_free"]] if "xnt_free" in prev else []
        ev_xnt = None
        for fic in range(8):
            ev_xnt = b.dma("y", xnt[:, fic * GT:(fic + 1) * GT],
                           io["xn_in"][g * GT:(g + 1) * GT, fic * 128:(fic + 1) * 128],
                           w=wv_xn if fic == 0 else (), transpose=True)
        h_done = None
        ci = 0
        for rg in range(8):
            for hh in range(4):
                for mc in range(4):
                    pi = ci % 2
                    wv = [ev_xnt, ev_bd] if ci == 0 else []
                    if "ht_free" in prev and ci == 0:
                        wv.append(prev["ht_free"])
                    if ci >= 2:
                        wv = [h_evs[ci - 2]]
                    tile_off = (hh * 8 + rg) * 512
                    ev_mm = b.t(lambda te, pi=pi, tile_off=tile_off, mc=mc, rg=rg: te.matmul(
                        ps_z[pi], wbd_s[:, tile_off + mc * 128: tile_off + mc * 128 + 128],
                        xnt[:, rg * GT:(rg + 1) * GT],
                        start=True, stop=True), w=wv, inc=1)
                    cc = hh * 32 + rg * 4 + mc
                    ev_h = b.s(lambda sc, pi=pi, cc=cc: sc.activation(
                        ht[:, cc * GT:(cc + 1) * GT], ps_z[pi], AF.Gelu), w=[ev_mm], inc=1)
                    if ci == 0:
                        h_evs = {}
                    h_evs[ci] = ev_h
                    ci += 1
        h_done = h_evs[ci - 1]
        prev["xnt_free"] = h_done
        if g == 0 and "dbg_ht" in io:
            h_done = b.dma("y", io["dbg_ht"], ht[:, 0:1024], w=[h_done])
            h_done = b.dma("y", io["dbg_ht2"][:, 0:512], ht[:, 4 * 512:5 * 512], w=[h_done])
            h_done = b.dma("y", io["dbg_ht2"][:, 512:1024], ht[:, 32 * 512:33 * 512], w=[h_done])
            b.dma("y", io["dbg_xnt4"], xnt[:, 0:4096], w=[h_done])

        ev_ep_done = []
        for dh in range(2):
            ev_o = [None] * 4
            ev_wcl = {}
            wv0 = [prev["wc_free"]] if "wc_free" in prev else [ev_wc]
            ev_wcl[0] = b.dma("y", wcs[0], wcb_d[0:128, :], w=wv0)
            foc_done = {}
            for foc in range(128):
                par = foc % 2
                if foc + 1 < 128:
                    wv_l = [foc_done[foc - 1]] if foc >= 1 else []
                    ev_wcl[foc + 1] = b.dma("y", wcs[(foc + 1) % 2],
                                            wcb_d[(foc + 1) * 128:(foc + 2) * 128, :], w=wv_l)
                for tsub in range(4):
                    wv = []
                    if foc == 0 and tsub == 0:
                        wv = [h_done, ev_wcl[0]] + ([prev["pso_free"]] if "pso_free" in prev else [])
                    elif tsub == 0:
                        wv = [ev_wcl[foc]]
                    ev_o[tsub] = b.t(lambda te, par=par, foc=foc, tsub=tsub, dh=dh: te.matmul(
                        ps_o[tsub], ht[:, foc * GT + tsub * 128: foc * GT + tsub * 128 + 128],
                        wcs[par][:, dh * 512:(dh + 1) * 512],
                        start=(foc == 0), stop=(foc == 127)),
                        w=wv, inc=(1 if tsub == 3 else 0))
                foc_done[foc] = ev_o[3]
            prev["wc_free"] = ("t", b.sems["t"].n)
            for tsub in range(4):
                db = dbuf[:, tsub * 1024 + dh * 512: tsub * 1024 + (dh + 1) * 512]
                wv_d = [ev_o[3]]
                if ("dbuf_free", tsub) in prev:
                    wv_d.append(prev.pop(("dbuf_free", tsub)))
                ev_d = b.v(lambda ve, db=db, tsub=tsub: ve.tensor_copy(db, ps_o[tsub]), w=wv_d, inc=1)
                if dh == 0:
                    prev[("dbuf0", tsub)] = ev_d
                else:
                    dfull = dbuf[:, tsub * 1024:(tsub + 1) * 1024]
                    wv_xt = [prev["xt_free"]] if "xt_free" in prev else []
                    ev_xt = b.dma("y", xt, io["x_in"][g * GT + tsub * 128: g * GT + (tsub + 1) * 128, :], w=wv_xt)
                    wv_st = [ev_d, prev[("dbuf0", tsub)]]
                    if "x2t_free" in prev:
                        wv_st.append(prev["x2t_free"])
                    a_c, b_c, ev_st = ln.stats(dfull, x2t, stat, 0, D, w=wv_st)
                    e = b.v(lambda ve, dfull=dfull: ve.tensor_scalar(
                        x2t, dfull, a_c, b_c, op0=ALU.mult, op1=ALU.add), w=[ev_st], inc=1)
                    if g == 0 and tsub == 0 and "dbg_dbuf" in io:
                        ev_st = b.dma("y", io["dbg_dbuf"], dfull, w=[ev_st])
                        b.wait("v", [ev_st])
                    ev_x2 = b.v(lambda ve: ve.tensor_tensor(x2t, x2t, xt, op=ALU.add), w=[e, ev_xt], inc=1)
                    ev_sx = b.dma("y", io["x_out"][g * GT + tsub * 128: g * GT + (tsub + 1) * 128, :],
                                  x2t, w=[ev_x2])
                    prev["x2t_free"] = ev_sx
                    prev["xt_free"] = ev_x2
                    prev[("dbuf_free", tsub)] = ev_x2
                    ev_ep_done.append(ev_sx)
            if dh == 0:
                prev["pso_free"] = prev[("dbuf0", 3)]
            else:
                prev["pso_free"] = ("v", b.sems["v"].n)
        prev["ht_free"] = ("t", b.sems["t"].n)

    b.wait("y", [("y", b.sems["y"].n)])
    b.wait("g", [("gd", b.sems["gd"].n)])


def build_p4_program():
    nc = bass.Bass("TRN2", target_bir_lowering=False)
    settle_scr = nc.dram_tensor("settle_scr", [8, 128], F32, kind="Internal").ap()
    io = {}
    io["x_in"] = nc.dram_tensor("x_in", [NT, D], F32, kind="ExternalInput").ap()
    io["xn_in"] = nc.dram_tensor("xn_in", [NT, D], BF16, kind="ExternalInput").ap()
    io["railmJ"] = nc.dram_tensor("railmJ", [16, 4096], F32, kind="ExternalInput").ap()
    io["railuJ"] = nc.dram_tensor("railuJ", [16, 16384], F32, kind="ExternalInput").ap()
    io["wcT"] = nc.dram_tensor("wcT", [DFF1, D], F32, kind="ExternalInput").ap()
    io["x_out"] = nc.dram_tensor("x_out", [NT, D], F32, kind="ExternalOutput").ap()
    io["wbd_d"] = nc.dram_tensor("wbd_d", [4096, 512], BF16, kind="Internal").ap()
    io["wcb_d"] = nc.dram_tensor("wcb_d", [DFF1, D], BF16, kind="Internal").ap()
    with ExitStack() as ctx:
        b = B(nc, ctx, sem_offset=24)
        rng = _pre_clean(nc, b, settle_scr=settle_scr)
        emit_p4(b, nc, ctx, io)
        blk = ctx.enter_context(nc.Block())
        b.finalize(blk)
    _post_clean(nc, rng)
    return nc


def emit_p3(b, nc, ctx, io):
    ln = LNHelper(b, expln=True)
    atbuf = io["atbuf"]

    wq_s = sbuf(nc, ctx, "p3_wq", [128, 8 * 1024], BF16)
    wk_s = sbuf(nc, ctx, "p3_wk", [128, 8 * 1024], BF16)
    wv_s = sbuf(nc, ctx, "p3_wv", [128, 8 * 1024], BF16)
    wo_s = sbuf(nc, ctx, "p3_wo", [128, 8 * 1024], BF16)
    canary = sbuf(nc, ctx, "p3_canary", [128, 256], F32) if "dbg_canary" in io else None
    k_s = sbuf(nc, ctx, "p3_ks", [128, 8 * 2048], BF16)
    v_s = sbuf(nc, ctx, "p3_vs", [128, 16 * 1040], BF16)
    q_t = sbuf(nc, ctx, "p3_qt", [128, 8 * 512], BF16)
    p_s = sbuf(nc, ctx, "p3_ps", [128, 2048], BF16)
    ptb = sbuf(nc, ctx, "p3_ptb", [128, 16 * 128], BF16)
    attn_tok = sbuf(nc, ctx, "p3_attn", [128, 1024], BF16)
    attnT = sbuf(nc, ctx, "p3_attnT", [128, 8 * 128], BF16)
    xnt = sbuf(nc, ctx, "p3_xnt", [128, 8 * 512], BF16)
    xt = sbuf(nc, ctx, "p3_xt", [128, 1024], F32)
    obuf = sbuf(nc, ctx, "p3_obuf", [128, 1024], F32)
    x3t = sbuf(nc, ctx, "p3_x3t", [128, 1024], F32)
    xn3t = sbuf(nc, ctx, "p3_xn3t", [128, 1024], F32)
    xn3b = sbuf(nc, ctx, "p3_xn3b", [128, 1024], BF16)
    stat = sbuf(nc, ctx, "p3_stat", [128, 16], F32)
    mstat = sbuf(nc, ctx, "p3_mstat", [128, 8], F32)
    zc = sbuf(nc, ctx, "p3_zc", [128, 2], F32)
    identb = sbuf(nc, ctx, "p3_ident", [128, 128], BF16)
    maskb = sbuf(nc, ctx, "p3_mask", [128, 128], BF16)
    bqt = sbuf(nc, ctx, "p3_bqt", [128, 8], F32)
    bkt = sbuf(nc, ctx, "p3_bkt", [128, 8], F32)
    ascol = sbuf(nc, ctx, "p3_ascol", [128, 1], F32)
    bvr = sbuf(nc, ctx, "p3_bvr", [1, 1024], F32)
    bvrb = sbuf(nc, ctx, "p3_bvrb", [1, 1024], BF16)
    bor = sbuf(nc, ctx, "p3_bor", [1, 1024], F32)
    borb = sbuf(nc, ctx, "p3_borb", [1, 1024], BF16)
    onesb = sbuf(nc, ctx, "p3_onesb", [1, 128], BF16)

    io_dbg_av_buf = sbuf(nc, ctx, "p3_dbgav", [128, 65], F32) if "dbg_av" in io else None
    ps_s = [psum(nc, ctx, f"p3_pss{i}", [128, 512], F32) for i in range(4)]
    pt_ps = psum(nc, ctx, "p3_ptps", [128, 1024], BF16)
    ps_av = psum(nc, ctx, "p3_psav", [128, 65], F32)
    ps_o = psum(nc, ctx, "p3_pso", [128, 1024], F32)

    ev_can = None
    if canary is not None:
        ev_can = b.v(lambda ve: ve.memset(canary, 7.0), inc=1)
    ev_w = None
    for dst, key in ((wq_s, "wqT1"), (wk_s, "wkT1"), (wv_s, "wvT1"), (wo_s, "woT1")):
        for fic in range(8):
            ev_w = b.dma("g", dst[:, fic * 1024:(fic + 1) * 1024],
                         io[key][fic * 128:(fic + 1) * 128, :],
                         w=[ev_can] if (ev_can is not None and ev_w is None) else ())
    ev_id = b.dma("y", identb, io["c_ident"])
    ev_mk = b.dma("y", maskb, io["c_mask"])
    ev_bq = b.dma("y", bqt, io["bq1t"])
    ev_bk = b.dma("y", bkt, io["bk1t"])
    ev_as = b.dma("y", ascol, io["ascol"])
    ev_bv = b.dma("y", bvr, io["bv1r"])
    ev_bo = b.dma("y", bor, io["bo1r"])
    ev_bvb = b.s(lambda sc: sc.copy(bvrb, bvr), w=[ev_bv], inc=1)
    ev_bob = b.s(lambda sc: sc.copy(borb, bor), w=[ev_bo], inc=1)
    ev_on = b.v(lambda ve: ve.memset(onesb, 1.0), inc=1)
    ev_bqs = b.v(lambda ve: ve.tensor_scalar(bqt, bqt, ascol, None, op0=ALU.mult),
                 w=[ev_bq, ev_as], inc=1)

    prev = {}
    for seq in range(2):
        base = seq * T
        ev_vz = b.v(lambda ve: ve.memset(v_s, 1.0),
                    w=[prev["vs_free"]] if "vs_free" in prev else [], inc=1)
        ev_kv_done = None
        for g in range(4):
            wv_xn = [prev["xnt_free"]] if "xnt_free" in prev else [ev_w]
            ev_xnt = None
            for fic in range(8):
                ev_xnt = b.dma("y", xnt[:, fic * 512:(fic + 1) * 512],
                               io["xn_in"][base + g * 512: base + (g + 1) * 512,
                                           fic * 128:(fic + 1) * 128],
                               w=wv_xn if fic == 0 else (), transpose=True)
            ev_kev = None
            for foc in range(8):
                wv0 = [ev_xnt, ev_bk]
                if g == 0 and "pss_free" in prev:
                    wv0 = wv0 + [prev["pss_free"]]
                if ev_kev is not None:
                    wv0 = [ev_kev]
                ev_mm = None
                for fic in range(8):
                    ev_mm = b.t(lambda te, foc=foc, fic=fic: te.matmul(
                        ps_s[0], wk_s[:, fic * 1024 + foc * 128: fic * 1024 + (foc + 1) * 128],
                        xnt[:, fic * 512:(fic + 1) * 512],
                        start=(fic == 0), stop=(fic == 7)),
                        w=wv0 if fic == 0 else (), inc=(1 if fic == 7 else 0))
                ev_kev = b.s(lambda sc, foc=foc, g=g: sc.activation(
                    k_s[:, foc * 2048 + g * 512: foc * 2048 + (g + 1) * 512], ps_s[0],
                    AF.Identity, bias=bkt[:, foc:foc + 1], scale=1.0), w=[ev_mm], inc=1)
            ev_vev = None
            for tb in range(4):
                gtb = g * 4 + tb
                ev_mmv = None
                wv0 = [ev_xnt, ev_bvb, ev_on, ev_vz]
                if g == 0 and "pss_free" in prev:
                    wv0 = wv0 + [prev["pss_free"]]
                if ev_vev is not None:
                    wv0 = [ev_vev]
                for fic in range(8):
                    for half in range(2):
                        ev_mmv = b.t(lambda te, fic=fic, tb=tb, half=half: te.matmul(
                            ps_s[1 + half],
                            xnt[:, fic * 512 + tb * 128: fic * 512 + (tb + 1) * 128],
                            wv_s[:, fic * 1024 + half * 512: fic * 1024 + (half + 1) * 512],
                            start=(fic == 0), stop=False),
                            w=wv0 if (fic == 0 and half == 0) else ())
                for half in range(2):
                    ev_mmv = b.t(lambda te, half=half: te.matmul(
                        ps_s[1 + half], onesb, bvrb[0:1, half * 512:(half + 1) * 512],
                        start=False, stop=True), inc=(1 if half == 1 else 0))
                ev_vev = None
                for half in range(2):
                    dsth = v_s[:, gtb * 1040 + half * 8 * 65: gtb * 1040 + (half + 1) * 8 * 65]
                    dsth = dsth.rearrange("p (h j) -> p h j", h=8)[:, :, 0:64]
                    ev_vev = b.v(lambda ve, dsth=dsth, half=half: ve.tensor_copy(
                        dsth, ps_s[1 + half].rearrange("p (h j) -> p h j", h=8)),
                        w=[ev_mmv], inc=1)
            ev_kv_done = ev_vev
            prev["xnt_free"] = ev_kv_done

        for qg in range(4):
            wv_xn = [prev["xnt_free"]] if "xnt_free" in prev else []
            ev_xnt = None
            for fic in range(8):
                ev_xnt = b.dma("y", xnt[:, fic * 512:(fic + 1) * 512],
                               io["xn_in"][base + qg * 512: base + (qg + 1) * 512,
                                           fic * 128:(fic + 1) * 128],
                               w=wv_xn if fic == 0 else (), transpose=True)
            ev_qev = None
            for foc in range(8):
                wv0 = [ev_xnt, ev_bqs]
                if "pso_free" in prev:
                    wv0.append(prev.pop("pso_free"))
                if ev_qev is not None:
                    wv0 = [ev_qev]
                ev_mm = None
                for fic in range(8):
                    ev_mm = b.t(lambda te, foc=foc, fic=fic: te.matmul(
                        ps_o[:, (foc % 2) * 512:(foc % 2) * 512 + 512],
                        wq_s[:, fic * 1024 + foc * 128: fic * 1024 + (foc + 1) * 128],
                        xnt[:, fic * 512:(fic + 1) * 512],
                        start=(fic == 0), stop=(fic == 7)),
                        w=wv0 if fic == 0 else (), inc=(1 if fic == 7 else 0))
                wv_q = [ev_mm]
                if "qt_free" in prev and foc == 0:
                    wv_q.append(prev.pop("qt_free"))
                ev_qev = b.s(lambda sc, foc=foc: sc.activation(
                    q_t[:, foc * 512:(foc + 1) * 512],
                    ps_o[:, (foc % 2) * 512:(foc % 2) * 512 + 512],
                    AF.Identity, bias=bqt[:, foc:foc + 1], scale=ascol), w=wv_q, inc=1)
            prev["xnt_free"] = ev_qev
            for qt in range(4):
                q0 = qg * 512 + qt * 128
                L = T - q0
                nkb = L // 128
                nbank = (L + 511) // 512
                ev_attn_done = prev.get("attn_done_ev")
                prev_exp = prev.get("prev_exp_ev")
                for h in range(16):
                    po = (h % 2) * 64
                    foc = h // 2
                    lhs_q = q_t[po:po + 64, foc * 512 + qt * 128: foc * 512 + (qt + 1) * 128]
                    ev_sc = []
                    for j in range(nbank):
                        klen = min(512, L - j * 512)
                        wv0 = []
                        if h == 0 and j == 0:
                            wv0 = [ev_qev, ev_kv_done]
                        if j == 0 and "pss_free" in prev:
                            wv0 = wv0 + [prev["pss_free"]]
                        ev_sc.append(b.t(lambda te, lhs_q=lhs_q, po=po, foc=foc, j=j,
                                         klen=klen, q0=q0: te.matmul(
                            ps_s[j][:, 0:klen], lhs_q,
                            k_s[po:po + 64, foc * 2048 + q0 + j * 512: foc * 2048 + q0 + j * 512 + klen],
                            start=True, stop=True), w=wv0, inc=1))
                    e = None
                    for j in range(nbank):
                        klen = min(512, L - j * 512)
                        wv_r = [ev_sc[j]]
                        if e is not None:
                            wv_r.append(e)
                        elif prev_exp is not None:
                            wv_r.append(prev_exp)
                        e = b.v(lambda ve, j=j, klen=klen: ve.reduce_max(
                            mstat[:, j:j + 1], ps_s[j][:, 0:klen], axis=AX.X, negate=True),
                            w=wv_r, inc=1)
                    wv_c = [e] + ([prev_exp] if prev_exp is not None else [])
                    e = b.v(lambda ve, nbank=nbank: ve.tensor_reduce(
                        mstat[:, 4:5], mstat[:, 0:nbank], axis=AX.X, op=ALU.min), w=wv_c, inc=1)
                    ev_negmax = e
                    ev_exp = None
                    for j in range(nbank):
                        klen = min(512, L - j * 512)
                        ev_exp = b.s(lambda sc, j=j, klen=klen: sc.activation(
                            p_s[:, j * 512: j * 512 + klen], ps_s[j][:, 0:klen],
                            AF.Exp, bias=mstat[:, 4:5], scale=1.0),
                            w=[ev_negmax] if j == 0 else (), inc=(1 if j == nbank - 1 else 0))
                    ev_mask = b.v(lambda ve: ve.tensor_tensor(
                        p_s[:, 0:128], p_s[:, 0:128], maskb, op=ALU.mult),
                        w=[ev_exp, ev_mk], inc=1)
                    ev_pt = None
                    nb8 = (nkb + 7) // 8
                    for bb in range(nb8):
                        n_in = min(8, nkb - bb * 8)
                        ev_tr = None
                        for kk in range(n_in):
                            kb = bb * 8 + kk
                            wv0 = [ev_mask] if (bb == 0 and kk == 0) else ()
                            if kk == 0 and bb > 0:
                                wv0 = [ev_pt]
                            ev_tr = b.t(lambda te, kb=kb, kk=kk: te.transpose(
                                pt_ps[:, kk * 128:(kk + 1) * 128],
                                p_s[:, kb * 128:(kb + 1) * 128], identb),
                                w=wv0, inc=(1 if kk == n_in - 1 else 0))
                        ev_pt = b.s(lambda sc, bb=bb, n_in=n_in: sc.copy(
                            ptb[:, bb * 1024: bb * 1024 + n_in * 128],
                            pt_ps[:, 0:n_in * 128]), w=[ev_tr], inc=1)
                    ev_av = None
                    for kb in range(nkb):
                        ktb = (q0 + kb * 128) // 128
                        wv0 = []
                        if kb == 0:
                            wv0 = [ev_pt, ev_kv_done] + ([prev["psav_free"]] if "psav_free" in prev else [])
                        ev_av = b.t(lambda te, ktb=ktb, kb=kb, h=h: te.matmul(
                            ps_av, ptb[:, kb * 128:(kb + 1) * 128],
                            v_s[:, ktb * 1040 + h * 65: ktb * 1040 + (h + 1) * 65],
                            start=(kb == 0), stop=(kb == nkb - 1)),
                            w=wv0, inc=(1 if kb == nkb - 1 else 0))
                    prev_exp = ev_exp
                    if seq == 0 and qg == 3 and qt == 3 and h == 0 and "dbg_p" in io:
                        ev_dp = b.dma("y", io["dbg_p"], p_s[:, 0:128], w=[ev_mask])
                        b.dma("y", io["dbg_mstat"], mstat, w=[ev_negmax])
                        ev_dav = b.v(lambda ve: ve.tensor_copy(io_dbg_av_buf, ps_av), w=[ev_av], inc=1)
                        b.dma("y", io["dbg_av"], io_dbg_av_buf, w=[ev_dav])
                    wv_z = [ev_av] + ([ev_attn_done] if ev_attn_done is not None else [])
                    e = b.v(lambda ve: ve.tensor_copy(zc[:, 0:1], ps_av[:, 64:65]),
                            w=wv_z, inc=1)
                    e = b.v(lambda ve: ve.reciprocal(zc[:, 1:2], zc[:, 0:1]), w=[e], inc=1)
                    wv_at = [e]
                    if h == 0 and "attn_free" in prev:
                        wv_at.append(prev.pop("attn_free"))
                    ev_attn_done = b.s(lambda sc, h=h: sc.activation(
                        attn_tok[:, h * 64:(h + 1) * 64], ps_av[:, 0:64],
                        AF.Identity, bias=0.0, scale=zc[:, 1:2]), w=wv_at, inc=1)
                    prev["pss_free"] = ev_exp
                    prev["psav_free"] = ev_attn_done
                prev["attn_done_ev"] = ev_attn_done
                prev["prev_exp_ev"] = prev_exp
                r0 = base + q0
                if seq == 0 and qg == 3 and qt == 3 and "dbg_attn" in io:
                    ev_attn_done = b.dma("y", io["dbg_attn"], attn_tok, w=[ev_attn_done])
                ev_ats = b.dma("y", atbuf[r0:r0 + 128, :], attn_tok, w=[ev_attn_done])
                prev["attn_free"] = ev_ats
                ev_att = None
                for fic in range(8):
                    wv0 = [ev_ats] if fic == 0 else ()
                    if fic == 0 and "attnT_free" in prev:
                        wv0 = [ev_ats, prev.pop("attnT_free")]
                    ev_att = b.dma("y", attnT[:, fic * 128:(fic + 1) * 128],
                                   atbuf[r0:r0 + 128, fic * 128:(fic + 1) * 128],
                                   w=wv0, transpose=True)
                ev_wo = None
                wv0 = [ev_att, ev_bob, ev_on, prev.pop("pso_free", None)]
                wv0 = [w for w in wv0 if w is not None]
                for fic in range(8):
                    for half in range(2):
                        ev_wo = b.t(lambda te, fic=fic, half=half: te.matmul(
                            ps_o[:, half * 512:(half + 1) * 512],
                            attnT[:, fic * 128:(fic + 1) * 128],
                            wo_s[:, fic * 1024 + half * 512: fic * 1024 + (half + 1) * 512],
                            start=(fic == 0), stop=False),
                            w=wv0 if (fic == 0 and half == 0) else ())
                for half in range(2):
                    ev_wo = b.t(lambda te, half=half: te.matmul(
                        ps_o[:, half * 512:(half + 1) * 512], onesb,
                        borb[0:1, half * 512:(half + 1) * 512],
                        start=False, stop=True), inc=(1 if half == 1 else 0))
                prev["attnT_free"] = ev_wo
                wv_ob = [ev_wo] + ([prev["obuf_free"]] if "obuf_free" in prev else [])
                ev_ob = b.v(lambda ve: ve.tensor_copy(obuf, ps_o), w=wv_ob, inc=1)
                prev["pso_free"] = ev_ob
                wv_xt = [prev["xt_free"]] if "xt_free" in prev else []
                ev_xt = b.dma("y", xt, io["x_in"][r0:r0 + 128, :], w=wv_xt)
                wv_st = [ev_ob] + ([prev["x3t_free"]] if "x3t_free" in prev else [])
                a_c, b_c, ev_st = ln.stats(obuf, x3t, stat, 0, D, w=wv_st)
                e = b.v(lambda ve: ve.tensor_scalar(x3t, obuf, a_c, b_c,
                                                    op0=ALU.mult, op1=ALU.add), w=[ev_st], inc=1)
                prev["obuf_free"] = e
                ev_x3 = b.v(lambda ve: ve.tensor_tensor(x3t, x3t, xt, op=ALU.add),
                            w=[e, ev_xt], inc=1)
                prev["xt_free"] = ev_x3
                ev_sx = b.dma("y", io["x_out"][r0:r0 + 128, :], x3t, w=[ev_x3])
                prev["x3t_free"] = ev_sx
                wv_st4 = [ev_x3] + ([prev["xn3t_free"]] if "xn3t_free" in prev else [])
                a3, b3c, ev_st4 = ln.stats(x3t, xn3t, stat, 8, D, w=wv_st4)
                ev_ap3 = b.v(lambda ve: ve.tensor_scalar(xn3t, x3t, a3, b3c,
                                                         op0=ALU.mult, op1=ALU.add), w=[ev_st4], inc=1)
                wv_cb3 = [ev_ap3] + ([prev["xn3b_free"]] if "xn3b_free" in prev else [])
                ev_cb3 = b.v(lambda ve: ve.tensor_copy(xn3b, xn3t), w=wv_cb3, inc=1)
                ev_sxn = b.dma("y", io["xn_out"][r0:r0 + 128, :], xn3b, w=[ev_cb3])
                prev["xn3t_free"] = ev_sxn
                prev["xn3b_free"] = ev_sxn
            prev["qt_free"] = ev_attn_done
        prev["vs_free"] = ("t", b.sems["t"].n)
        if seq == 1 and "dbg_wq" in io:
            b.dma("y", io["dbg_wq"], wq_s, w=[prev["vs_free"]])
            b.dma("y", io["dbg_wo"], wo_s, w=[prev["vs_free"]])
        if seq == 1 and "dbg_canary" in io:
            b.dma("y", io["dbg_canary"], canary, w=[prev["vs_free"]])
        if seq == 1 and "dbg_sm" in io:
            b.dma("y", io["dbg_sm"][:, 0:128], identb, w=[prev["vs_free"]])
            b.dma("y", io["dbg_sm"][:, 128:256], maskb)
            b.dma("y", io["dbg_smf"][:, 0:8], bqt, slow=True)
            b.dma("y", io["dbg_smf"][:, 8:16], bkt, slow=True)
            b.dma("y", io["dbg_smf"][:, 16:17], ascol, slow=True)
            b.dma("y", io["dbg_kv"][:, 0:2048], k_s[:, 0:2048])
            b.dma("y", io["dbg_kv"][:, 2048:4096], v_s[:, 0:2048])

    b.wait("y", [("y", b.sems["y"].n)])
    b.wait("g", [("gd", b.sems["gd"].n)])


def build_p3_program():
    nc = bass.Bass("TRN2", target_bir_lowering=False)
    settle_scr = nc.dram_tensor("settle_scr", [8, 128], F32, kind="Internal").ap()
    io = {}
    io["x_in"] = nc.dram_tensor("x_in", [NT, D], F32, kind="ExternalInput").ap()
    io["xn_in"] = nc.dram_tensor("xn_in", [NT, D], BF16, kind="ExternalInput").ap()
    for k in ("wqT1", "wkT1", "wvT1", "woT1"):
        io[k] = nc.dram_tensor(k, [D, D], F32, kind="ExternalInput").ap()
    io["bq1t"] = nc.dram_tensor("bq1t", [128, 8], F32, kind="ExternalInput").ap()
    io["bk1t"] = nc.dram_tensor("bk1t", [128, 8], F32, kind="ExternalInput").ap()
    io["bv1r"] = nc.dram_tensor("bv1r", [1, D], F32, kind="ExternalInput").ap()
    io["bo1r"] = nc.dram_tensor("bo1r", [1, D], F32, kind="ExternalInput").ap()
    io["ascol"] = nc.dram_tensor("ascol", [128, 1], F32, kind="ExternalInput").ap()
    io["c_ident"] = nc.dram_tensor("c_ident", [128, 128], BF16, kind="ExternalInput").ap()
    io["c_mask"] = nc.dram_tensor("c_mask", [128, 128], BF16, kind="ExternalInput").ap()
    io["x_out"] = nc.dram_tensor("x_out", [NT, D], F32, kind="ExternalOutput").ap()
    io["xn_out"] = nc.dram_tensor("xn_out", [NT, D], BF16, kind="ExternalOutput").ap()
    io["atbuf"] = nc.dram_tensor("atbuf", [NT, D], BF16, kind="Internal").ap()
    with ExitStack() as ctx:
        b = B(nc, ctx, sem_offset=16)
        rng = _pre_clean(nc, b, settle_scr=settle_scr)
        emit_p3(b, nc, ctx, io)
        blk = ctx.enter_context(nc.Block())
        b.finalize(blk)
    _post_clean(nc, rng)
    return nc


def build_p2_program():
    nc = bass.Bass("TRN2", target_bir_lowering=False)
    settle_scr = nc.dram_tensor("settle_scr", [8, 128], F32, kind="Internal").ap()
    io = {}
    io["x_in"] = nc.dram_tensor("x_in", [NT, D], F32, kind="ExternalInput").ap()
    io["xn_in"] = nc.dram_tensor("xn_in", [NT, D], BF16, kind="ExternalInput").ap()
    io["w1T"] = nc.dram_tensor("w1T", [D, DFF0], F32, kind="ExternalInput").ap()
    io["b1t"] = nc.dram_tensor("b1t", [128, 32], F32, kind="ExternalInput").ap()
    io["w2T"] = nc.dram_tensor("w2T", [DFF0, D], F32, kind="ExternalInput").ap()
    io["b2r"] = nc.dram_tensor("b2r", [1, D], F32, kind="ExternalInput").ap()
    io["x_out"] = nc.dram_tensor("x_out", [NT, D], F32, kind="ExternalOutput").ap()
    io["xn_out"] = nc.dram_tensor("xn_out", [NT, D], BF16, kind="ExternalOutput").ap()
    with ExitStack() as ctx:
        b = B(nc, ctx, sem_offset=8)
        rng = _pre_clean(nc, b, settle_scr=settle_scr)
        emit_p2(b, nc, ctx, io)
        blk = ctx.enter_context(nc.Block())
        b.finalize(blk)
    _post_clean(nc, rng)
    return nc


def build_p1_program():
    nc = bass.Bass("TRN2", target_bir_lowering=False)
    settle_scr = nc.dram_tensor("settle_scr", [8, 128], F32, kind="Internal").ap()
    io = {}
    io["x_in"] = nc.dram_tensor("x_in", [NT, D], F32, kind="ExternalInput").ap()
    for k in ("w0qT", "w0vT", "w0gT", "w0oT"):
        io[k] = nc.dram_tensor(k, [D, D], F32, kind="ExternalInput").ap()
    io["g0full"] = nc.dram_tensor("g0full", [128, D], F32, kind="ExternalInput").ap()
    io["b0full"] = nc.dram_tensor("b0full", [128, D], F32, kind="ExternalInput").ap()
    io["decay_row"] = nc.dram_tensor("decay_row", [1, 16], F32, kind="ExternalInput").ap()
    io["decay_col"] = nc.dram_tensor("decay_col", [16, 1], F32, kind="ExternalInput").ap()
    io["c_uptri"] = nc.dram_tensor("c_uptri", [128, 128], F32, kind="ExternalInput").ap()
    io["x_out"] = nc.dram_tensor("x_out", [NT, D], F32, kind="ExternalOutput").ap()
    io["xn_out"] = nc.dram_tensor("xn_out", [NT, D], BF16, kind="ExternalOutput").ap()
    io["xn0buf"] = nc.dram_tensor("xn0buf", [NT, D], BF16, kind="Internal").ap()
    io["y2buf"] = nc.dram_tensor("y2buf", [NT, D], BF16, kind="Internal").ap()
    io["r_dram"] = nc.dram_tensor("r_dram", [16, 4096], F32, kind="Internal").ap()
    io["lamp_df"] = nc.dram_tensor("lamp_df", [16, 130], F32, kind="Internal").ap()
    io["lamp_db"] = nc.dram_tensor("lamp_db", [16, 130], BF16, kind="Internal").ap()
    io["l128_d"] = nc.dram_tensor("l128_d", [16, 128], F32, kind="Internal").ap()
    with ExitStack() as ctx:
        b = B(nc, ctx, sem_offset=0)
        rng = _pre_clean(nc, b, settle_scr=settle_scr)
        emit_p1(b, nc, ctx, io)
        blk = ctx.enter_context(nc.Block())
        b.finalize(blk)
    _post_clean(nc, rng)
    return nc



def _pre_clean(nc, b, settle_scr=None):
    nums = sorted(s.h.num for s in b.sems.values())
    junk = nc.alloc_semaphore("junk_settle")
    rng = range(nums[0], max(nums[-1], junk.num) + 1)
    nc.gpsimd.dma_reset(rng)
    nc.gpsimd.sem_clear(rng)
    engs = (nc.sync, nc.scalar, nc.vector, nc.tensor, nc.gpsimd)
    for eng in engs:
        for _ in range(20):
            eng.nop(cycle_cnt=4000)
    if settle_scr is not None:
        n = 0
        for q in (nc.sync, nc.gpsimd):
            for i in range(4):
                q.dma_start(out=settle_scr[i:i + 1, :], in_=settle_scr[4 + i:5 + i, :]
                            ).then_inc(junk, 16)
                n += 16
                q.wait_ge(junk, n)
        nc.gpsimd.dma_reset(rng)
        nc.gpsimd.sem_clear(rng)
        for eng in engs:
            for _ in range(20):
                eng.nop(cycle_cnt=4000)
    return rng


def _post_clean(nc, rng):
    nc.all_engine_barrier()
    nc.gpsimd.dma_reset(rng)
    nc.gpsimd.sem_clear(rng)
    nc.all_engine_barrier()


DELAY_NOPS = 300
DELAY_ENGS = None
DBG_HANDOFF = False


def build_fused_program(upto=4, sem_off=32, frm=1):
    nc = bass.Bass("TRN2", target_bir_lowering=False)
    settle_scr = nc.dram_tensor("settle_scr", [8, 128], F32, kind="Internal").ap()
    io = {}
    io["x0"] = nc.dram_tensor("x0", [NT, D], F32, kind="ExternalInput").ap()
    for k in ("w0qT", "w0vT", "w0gT", "w0oT"):
        io[k] = nc.dram_tensor(k, [D, D], F32, kind="ExternalInput").ap()
    io["g0full"] = nc.dram_tensor("g0full", [128, D], F32, kind="ExternalInput").ap()
    io["b0full"] = nc.dram_tensor("b0full", [128, D], F32, kind="ExternalInput").ap()
    io["decay_row"] = nc.dram_tensor("decay_row", [1, 16], F32, kind="ExternalInput").ap()
    io["decay_col"] = nc.dram_tensor("decay_col", [16, 1], F32, kind="ExternalInput").ap()
    io["c_uptri"] = nc.dram_tensor("c_uptri", [128, 128], F32, kind="ExternalInput").ap()
    io["w1T"] = nc.dram_tensor("w1T", [D, DFF0], F32, kind="ExternalInput").ap()
    io["b1t"] = nc.dram_tensor("b1t", [128, 32], F32, kind="ExternalInput").ap()
    io["w2T"] = nc.dram_tensor("w2T", [DFF0, D], F32, kind="ExternalInput").ap()
    io["b2r"] = nc.dram_tensor("b2r", [1, D], F32, kind="ExternalInput").ap()
    for k in ("wqT1", "wkT1", "wvT1", "woT1"):
        io[k] = nc.dram_tensor(k, [D, D], F32, kind="ExternalInput").ap()
    io["bq1t"] = nc.dram_tensor("bq1t", [128, 8], F32, kind="ExternalInput").ap()
    io["bk1t"] = nc.dram_tensor("bk1t", [128, 8], F32, kind="ExternalInput").ap()
    io["bv1r"] = nc.dram_tensor("bv1r", [1, D], F32, kind="ExternalInput").ap()
    io["bo1r"] = nc.dram_tensor("bo1r", [1, D], F32, kind="ExternalInput").ap()
    io["ascol"] = nc.dram_tensor("ascol", [128, 1], F32, kind="ExternalInput").ap()
    io["c_ident"] = nc.dram_tensor("c_ident", [128, 128], BF16, kind="ExternalInput").ap()
    io["c_mask"] = nc.dram_tensor("c_mask", [128, 128], BF16, kind="ExternalInput").ap()
    io["railmJ"] = nc.dram_tensor("railmJ", [16, 4096], F32, kind="ExternalInput").ap()
    io["railuJ"] = nc.dram_tensor("railuJ", [16, 16384], F32, kind="ExternalInput").ap()
    io["wcT"] = nc.dram_tensor("wcT", [DFF1, D], F32, kind="ExternalInput").ap()
    out_final = nc.dram_tensor("out", [NT, D], F32, kind="ExternalOutput").ap()
    xbuf = [nc.dram_tensor(f"xbuf{i}", [NT, D], F32, kind="Internal").ap() for i in range(3)]
    xnbuf = [nc.dram_tensor(f"xnbuf{i}", [NT, D], BF16, kind="Internal").ap() for i in range(3)]
    io["xn0buf"] = nc.dram_tensor("xn0buf", [NT, D], BF16, kind="Internal").ap()
    io["y2buf"] = nc.dram_tensor("y2buf", [NT, D], BF16, kind="Internal").ap()
    io["r_dram"] = nc.dram_tensor("r_dram", [16, 4096], F32, kind="Internal").ap()
    io["lamp_df"] = nc.dram_tensor("lamp_df", [16, 130], F32, kind="Internal").ap()
    io["lamp_db"] = nc.dram_tensor("lamp_db", [16, 130], BF16, kind="Internal").ap()
    io["l128_d"] = nc.dram_tensor("l128_d", [16, 128], F32, kind="Internal").ap()
    io["atbuf"] = nc.dram_tensor("atbuf", [NT, D], BF16, kind="Internal").ap()
    io["wbd_d"] = nc.dram_tensor("wbd_d", [4096, 512], BF16, kind="Internal").ap()
    io["wcb_d"] = nc.dram_tensor("wcb_d", [DFF1, D], BF16, kind="Internal").ap()

    if frm > 1:
        io["xn_ext"] = nc.dram_tensor("xn_ext", [NT, D], BF16, kind="ExternalInput").ap()
    if DBG_HANDOFF:
        io["dbg_xb1"] = nc.dram_tensor("dbg_xb1", [NT, D], F32, kind="ExternalOutput").ap()
        io["dbg_xnb1"] = nc.dram_tensor("dbg_xnb1", [NT, D], BF16, kind="ExternalOutput").ap()
    flush_d = nc.dram_tensor("flush_d", [128, 128], BF16, kind="Internal").ap()
    with ExitStack() as ctx:
        b = B(nc, ctx, sem_offset=sem_off)
        b.flush_src = flush_d
        b.flush_dst = sbuf(nc, ctx, "flush_sb", [128, 128], BF16)
        rng = _pre_clean(nc, b, settle_scr=settle_scr)
        io1 = dict(io); io1["x_in"] = io["x0"]
        io1["x_out"] = out_final if upto == 1 else xbuf[0]; io1["xn_out"] = xnbuf[0]
        if frm <= 1:
            with ExitStack() as c1:
                emit_p1(b, nc, c1, io1)
                b.bar(delay_nops=DELAY_NOPS)
        if upto >= 2 and frm <= 2:
            io2 = dict(io)
            io2["x_in"] = xbuf[0] if frm <= 1 else io["x0"]
            io2["xn_in"] = xnbuf[0] if frm <= 1 else io["xn_ext"]
            io2["x_out"] = out_final if upto == 2 else xbuf[1]; io2["xn_out"] = xnbuf[1]
            with ExitStack() as c2:
                emit_p2(b, nc, c2, io2)
                b.bar(delay_nops=DELAY_NOPS)
        if upto >= 3 and frm <= 3:
            io3 = dict(io)
            io3["x_in"] = xbuf[1] if frm <= 2 else io["x0"]
            io3["xn_in"] = xnbuf[1] if frm <= 2 else io["xn_ext"]
            io3["x_out"] = out_final if upto == 3 else xbuf[2]; io3["xn_out"] = xnbuf[2]
            with ExitStack() as c3:
                emit_p3(b, nc, c3, io3)
                b.bar(delay_nops=DELAY_NOPS)
        if "dbg_xb1" in io:
            b.wait("y", [("gd", b.sems["gd"].n), ("y", b.sems["y"].n)])
            b.dma("y", io["dbg_xb1"], xbuf[1])
            b.dma("y", io["dbg_xnb1"], xnbuf[1])
        if upto >= 4:
            io4 = dict(io)
            io4["x_in"] = xbuf[2] if frm <= 3 else io["x0"]
            io4["xn_in"] = xnbuf[2] if frm <= 3 else io["xn_ext"]
            io4["x_out"] = out_final
            with ExitStack() as c4:
                emit_p4(b, nc, c4, io4)
                b.bar()
        blk = ctx.enter_context(nc.Block())
        b.finalize(blk)
    _post_clean(nc, rng)
    return nc


_PROG_CACHE = {}


def _get_prog(name, builder):
    if name not in _PROG_CACHE:
        _PROG_CACHE[name] = builder()
    return _PROG_CACHE[name]


_WARMED = set()


def _launch(nc, in_maps, warm=True):
    if warm and id(nc) not in _WARMED:
        run_bass_kernel_spmd(nc, in_maps, core_ids=list(range(NCORE)))
        _WARMED.add(id(nc))
    res = run_bass_kernel_spmd(nc, in_maps, core_ids=list(range(NCORE)))
    return res.results


def _filter(nc, m):
    names = set()
    for alloc in nc.m.functions[0].allocations:
        try:
            if alloc.kind == "ExternalInput":
                names.add(alloc.memorylocations[0].name)
        except AttributeError:
            pass
    return {k: v for k, v in m.items() if k in names}


FUSED = True


def kernel(**inputs):
    staged = stage_inputs(inputs)
    if FUSED:
        pf = _get_prog("fused", build_fused_program)
        maps = []
        for m in staged:
            mm = dict(m)
            mm["x0"] = mm.pop("x_in")
            maps.append(_filter(pf, mm))
        res = _launch(pf, maps, warm=False)
        out = np.concatenate([r["out"] for r in res], axis=0)
        return out.reshape(16, T, D)

    p1 = _get_prog("p1", build_p1_program)
    p2 = _get_prog("p2", build_p2_program)
    p3 = _get_prog("p3", build_p3_program)
    p4 = _get_prog("p4", build_p4_program)

    r1 = _launch(p1, [_filter(p1, m) for m in staged])
    for m, r in zip(staged, r1):
        m["x_in"] = r["x_out"]
        m["xn_in"] = r["xn_out"]
    r2 = _launch(p2, [_filter(p2, m) for m in staged])
    for m, r in zip(staged, r2):
        m["x_in"] = r["x_out"]
        m["xn_in"] = r["xn_out"]
    r3 = _launch(p3, [_filter(p3, m) for m in staged])
    for m, r in zip(staged, r3):
        m["x_in"] = r["x_out"]
        m["xn_in"] = r["xn_out"]
    r4 = _launch(p4, [_filter(p4, m) for m in staged])
    out = np.concatenate([r["x_out"] for r in r4], axis=0)
    return out.reshape(16, T, D)


# version 07
# speedup vs baseline: 1.0866x; 1.0866x over previous
import numpy as np
import ml_dtypes
from contextlib import ExitStack
import concourse.bass as bass
import concourse.mybir as mybir
from concourse.bass_utils import run_bass_kernel_spmd

F32 = mybir.dt.float32
BF16 = mybir.dt.bfloat16
AF = mybir.ActivationFunctionType
ALU = mybir.AluOpType
AX = mybir.AxisListType
BF = ml_dtypes.bfloat16

D = 1024
T = 2048
NCORE = 8
BPC = 2
NT = BPC * T
NTB = NT // 128
DFF0 = 4096
NH = 16
HD = 64
RAILS, RAIL_W, RMULT, RHEADS = 64, 16, 4, 4
DFF1 = 16384
LN_EPS = 1e-5
QEPS = 1e-5


class Sem:
    __slots__ = ("h", "n")
    def __init__(self, h):
        self.h = h
        self.n = 0


class B:
    ENGS = ("y", "g", "v", "s", "t")

    def __init__(self, nc, ctx, sem_offset=0):
        self.nc = nc
        self.L = {k: [] for k in self.ENGS}
        self.sems = {}
        for i in range(sem_offset):
            nc.alloc_semaphore(f"skip{i}")
        for name in ("y", "gd", "gc", "v", "s", "t"):
            self.sems[name] = Sem(ctx.enter_context(nc.semaphore("sem_" + name)))
        self.flush_src = None
        self.flush_dst = None

    def _op(self, eng_key, fn, w, inc_name, inc_by):
        ev = None
        if inc_by:
            sem = self.sems[inc_name]
            sem.n += inc_by
            ev = (inc_name, sem.n)
            tgt = (sem.h, sem.n)
        waits = [(self.sems[n].h, v) for (n, v) in w]

        def run(eng):
            for (h, v) in waits:
                eng.wait_ge(h, v)
            ins = fn(eng)
            if inc_by:
                ins.then_inc(tgt[0], inc_by)
        self.L[eng_key].append(run)
        return ev

    def t(self, fn, w=(), inc=0):
        return self._op("t", fn, w, "t", inc)

    def v(self, fn, w=(), inc=0):
        return self._op("v", fn, w, "v", inc)

    def s(self, fn, w=(), inc=0):
        return self._op("s", fn, w, "s", inc)

    def g(self, fn, w=(), inc=0):
        return self._op("g", fn, w, "gc", inc)

    def dma(self, q, out, in_, w=(), transpose=False, slow=False):
        sem_name = "y" if q == "y" else "gd"
        kw = {}
        if transpose:
            kw["transpose"] = True
        if slow:
            kw["allow_slow_non_contiguous"] = True

        def fn(eng):
            return eng.dma_start(out=out, in_=in_, **kw)
        return self._op(q if q == "y" else "g", fn, w, sem_name, 16)

    def wait(self, eng_key, w):
        waits = [(self.sems[n].h, v) for (n, v) in w]

        def run(eng):
            for (h, v) in waits:
                eng.wait_ge(h, v)
        self.L[eng_key].append(run)

    def bar(self, delay_nops=0):
        if self.flush_src is not None:
            self.dma("y", self.flush_dst, self.flush_src, transpose=True)
        wl = [(n, self.sems[n].n) for n in ("y", "gd", "gc", "v", "s", "t")
              if self.sems[n].n > 0]
        for k in self.ENGS:
            self.wait(k, wl)
            if delay_nops and (DELAY_ENGS is None or k in DELAY_ENGS):
                def run(eng, n=delay_nops):
                    for _ in range(n):
                        eng.nop(cycle_cnt=4000)
                self.L[k].append(run)

    def finalize(self, blk):
        for key, attr in (("y", "sync"), ("g", "gpsimd"), ("v", "vector"),
                          ("s", "scalar"), ("t", "tensor")):
            ops = self.L[key]

            def make(ops):
                def run(eng):
                    for f in ops:
                        f(eng)
                return run
            getattr(blk, attr)(make(ops))


def sbuf(nc, ctx, name, shape, dtype):
    return ctx.enter_context(nc.sbuf_tensor(name, shape, dtype))[:]


def psum(nc, ctx, name, shape, dtype=F32):
    return ctx.enter_context(nc.psum_tensor(name, shape, dtype))[:]


class LNHelper:
    def __init__(self, b, expln=False):
        self.b = b
        self.expln = expln

    def stats(self, x_ap, sq_scratch, stat, col0, width, w=()):
        b = self.b
        c = lambda i: stat[:, col0 + i:col0 + i + 1]
        e = b.v(lambda ve: ve.tensor_tensor(sq_scratch, x_ap, x_ap, op=ALU.mult), w=w, inc=1)
        e = b.v(lambda ve: ve.reduce_sum(c(0), sq_scratch, axis=AX.X), w=[e], inc=1)
        e2 = b.v(lambda ve: ve.reduce_sum(c(1), x_ap, axis=AX.X), w=[e], inc=1)
        e = b.v(lambda ve: ve.tensor_scalar(c(2), c(1), 1.0 / width, None, op0=ALU.mult), w=[e2], inc=1)
        e = b.v(lambda ve: ve.tensor_scalar(c(3), c(0), 1.0 / width, None, op0=ALU.mult), w=[e], inc=1)
        e = b.v(lambda ve: ve.tensor_tensor(c(4), c(2), c(2), op=ALU.mult), w=[e], inc=1)
        e = b.v(lambda ve: ve.tensor_tensor(c(3), c(3), c(4), op=ALU.subtract), w=[e], inc=1)
        e = b.v(lambda ve: ve.tensor_scalar(c(3), c(3), LN_EPS, None, op0=ALU.add), w=[e], inc=1)
        if self.expln:
            e = b.s(lambda sc: sc.activation(c(4), c(3), AF.Ln), w=[e], inc=1)
            e = b.s(lambda sc: sc.activation(c(6), c(4), AF.Exp, scale=-0.5), w=[e], inc=1)
        else:
            e = b.s(lambda sc: sc.activation(c(4), c(3), AF.Sqrt), w=[e], inc=1)
            e = b.v(lambda ve: ve.reciprocal(c(6), c(4)), w=[e], inc=1)
        e = b.v(lambda ve: ve.tensor_tensor(c(5), c(2), c(6), op=ALU.mult), w=[e], inc=1)
        e = b.v(lambda ve: ve.tensor_scalar(c(5), c(5), -1.0, None, op0=ALU.mult), w=[e], inc=1)
        return (c(6), c(5), e)


def stage_inputs(inputs):
    asc = np.ascontiguousarray
    x = np.asarray(inputs["L0_attn_x"], dtype=np.float32)

    shared = {}
    shared["w0qT"] = asc(np.asarray(inputs["L0_attn_Wq"]).T)
    shared["w0vT"] = asc(np.asarray(inputs["L0_attn_Wv"]).T)
    shared["w0gT"] = asc(np.asarray(inputs["L0_attn_Wg"]).T)
    shared["w0oT"] = asc(np.asarray(inputs["L0_attn_Wo"]).T)
    shared["g0full"] = asc(np.broadcast_to(np.asarray(inputs["L0_attn_ln_gamma"])[None, :], (128, D)).astype(np.float32))
    shared["b0full"] = asc(np.broadcast_to(np.asarray(inputs["L0_attn_ln_beta"])[None, :], (128, D)).astype(np.float32))
    shared["decay_row"] = asc(np.asarray(inputs["L0_attn_decay"], dtype=np.float32)[None, :])
    shared["decay_col"] = asc(np.asarray(inputs["L0_attn_decay"], dtype=np.float32)[:, None])

    shared["w1T"] = asc(np.asarray(inputs["L0_ffn_W1"]).T)
    shared["b1t"] = asc(np.asarray(inputs["L0_ffn_b1"], dtype=np.float32).reshape(32, 128).T)
    shared["w2T"] = asc(np.asarray(inputs["L0_ffn_W2"]).T)
    shared["b2r"] = asc(np.asarray(inputs["L0_ffn_b2"], dtype=np.float32)[None, :])

    shared["wqT1"] = asc(np.asarray(inputs["L1_attn_Wq"]).T)
    shared["wkT1"] = asc(np.asarray(inputs["L1_attn_Wk"]).T)
    shared["wvT1"] = asc(np.asarray(inputs["L1_attn_Wv"]).T)
    shared["woT1"] = asc(np.asarray(inputs["L1_attn_Wo"]).T)
    shared["bq1t"] = asc(np.asarray(inputs["L1_attn_bq"], dtype=np.float32).reshape(8, 128).T)
    shared["bk1t"] = asc(np.asarray(inputs["L1_attn_bk"], dtype=np.float32).reshape(8, 128).T)
    shared["bv1r"] = asc(np.asarray(inputs["L1_attn_bv"], dtype=np.float32)[None, :])
    shared["bo1r"] = asc(np.asarray(inputs["L1_attn_bo"], dtype=np.float32)[None, :])
    shared["ascol"] = asc(np.broadcast_to(np.asarray(inputs["L1_attn_attn_scale"], dtype=np.float32).reshape(1, 1), (128, 1)))

    shared["railmJ"] = asc(np.asarray(inputs["L1_ffn_rail_maker"]).transpose(3, 0, 1, 2).reshape(16, 4096))
    shared["railuJ"] = asc(np.asarray(inputs["L1_ffn_rail_up_proj"]).transpose(3, 0, 1, 2).reshape(16, 16384))
    shared["wcT"] = asc(np.asarray(inputs["L1_ffn_w_cproj"]).T)

    shared["c_ones"] = np.ones((1, 512), dtype=BF)
    shared["c_ident"] = asc(np.eye(128, dtype=np.float32).astype(BF))
    qi = np.arange(128)[:, None]
    ki = np.arange(128)[None, :]
    allowed = ((ki // 64) > (qi // 64)) | (((ki // 64) == (qi // 64)) & (ki <= qi))
    shared["c_mask"] = asc(allowed.astype(np.float32).astype(BF))
    shared["c_uptri"] = asc((np.arange(128)[None, :] >= np.arange(128)[:, None]).astype(np.float32))

    in_maps = []
    for c in range(NCORE):
        m = dict(shared)
        m["x_in"] = asc(x[c * BPC:(c + 1) * BPC].reshape(NT, D))
        in_maps.append(m)
    return in_maps


def emit_quant(b, nc, ctx, io, dst_w1q, dst_w2q, ones_col_f32, ones_row_f32):
    wtile = [sbuf(nc, ctx, f"qt_w{i}", [128, 4096], F32) for i in range(2)]
    sgn = sbuf(nc, ctx, "qt_sgn", [128, 4096], F32)
    m01 = sbuf(nc, ctx, "qt_m01", [128, 4096], F32)
    pp = sbuf(nc, ctx, "qt_pp", [128, 48], F32)
    acc = sbuf(nc, ctx, "qt_acc", [128, 4], F32)
    gam = sbuf(nc, ctx, "qt_gam", [1, 4], F32)
    taucol = sbuf(nc, ctx, "qt_tau", [128, 4], F32)
    ps_g = psum(nc, ctx, "qt_psg", [1, 2], F32)
    ps_b = psum(nc, ctx, "qt_psb", [128, 2], F32)

    specs = [("w1", io["w1T"], 8, 4096, dst_w1q, 0),
             ("w2", io["w2T"], 32, 1024, dst_w2q, 1)]
    col = 0
    tile_cols = {}
    lin = []
    for name, src, ntile, width, dst, gi in specs:
        for i in range(ntile):
            par = len(lin) % 2
            wv = [lin[-2]] if len(lin) >= 2 else []
            evl = b.dma("y", wtile[par][:, 0:width], src[i * 128:(i + 1) * 128, :], w=wv)
            ev = b.v(lambda ve, p=par, wd=width, cc=col: ve.reduce_sum(
                pp[:, cc:cc + 1], wtile[p][:, 0:wd], axis=AX.X,
                apply_absolute_value=True), w=[evl], inc=1)
            tile_cols[(name, i)] = ev
            lin.append(ev)
            col += 1
    ev = b.v(lambda ve: ve.reduce_sum(acc[:, 0:1], pp[:, 0:8], axis=AX.X), inc=1)
    ev2 = b.v(lambda ve: ve.reduce_sum(acc[:, 1:2], pp[:, 8:40], axis=AX.X), w=[ev], inc=1)
    evm = b.t(lambda te: te.matmul(ps_g, ones_col_f32, acc[:, 0:2], start=True, stop=True),
              w=[ev2], inc=1)
    ev3 = b.s(lambda sc: sc.copy(gam[:, 0:2], ps_g), w=[evm], inc=1)
    ev4 = b.s(lambda sc: sc.activation(gam[:, 2:4], gam[:, 0:2], AF.Copy,
                                       bias=QEPS, scale=1.0 / (4096 * 1024)), w=[ev3], inc=1)
    evb = b.t(lambda te: te.matmul(ps_b, ones_row_f32, gam[0:1, 2:4], start=True, stop=True),
              w=[ev4], inc=1)
    evt0 = b.s(lambda sc: sc.activation(taucol[:, 0:2], ps_b, AF.Copy, bias=0.0, scale=0.5), w=[evb], inc=1)
    evt = b.s(lambda sc: sc.activation(taucol[:, 2:4], taucol[:, 0:2], AF.Copy, bias=0.0, scale=-1.0), w=[evt0], inc=1)

    prev_q = {}
    last_q = None
    lin2 = []
    for name, src, ntile, width, dst, gi in specs:
        for i in range(ntile):
            par = len(lin2) % 2
            wv = [lin2[-2]] if len(lin2) >= 2 else [lin[-2 + len(lin2)]]
            evl = b.dma("y", wtile[par][:, 0:width], src[i * 128:(i + 1) * 128, :], w=wv + [evt])
            wv_s = [evl] + ([last_q] if last_q is not None else [])
            evs = b.v(lambda ve, p=par, wd=width, g=gi: ve.tensor_scalar(
                sgn[:, 0:wd], wtile[p][:, 0:wd], taucol[:, g:g + 1], None,
                op0=ALU.is_gt), w=wv_s, inc=1)
            evm1 = b.v(lambda ve, p=par, wd=width, g=gi: ve.tensor_scalar(
                m01[:, 0:wd], wtile[p][:, 0:wd], taucol[:, 2 + g:3 + g], None,
                op0=ALU.is_lt), w=[evl], inc=1)
            if name == "w1":
                dslice = dst[:, i * 4096:(i + 1) * 4096]
            else:
                dslice = dst[:, i * 1024:(i + 1) * 1024]
            evq = b.v(lambda ve, ds=dslice, wd=width: ve.tensor_tensor(
                ds, sgn[:, 0:wd], m01[:, 0:wd], op=ALU.subtract), w=[evm1], inc=1)
            prev_q[(name, i)] = evq
            last_q = evq
            lin2.append(evq)
    return prev_q[("w2", 31)]


def emit_p2(b, nc, ctx, io):
    ln = LNHelper(b)
    ones_col = sbuf(nc, ctx, "p2_onescol", [128, 1], F32)
    ones_row = sbuf(nc, ctx, "p2_onesrow", [1, 128], F32)
    ev_oc = b.v(lambda ve: ve.memset(ones_col, 1.0), inc=1)
    ev_or = b.v(lambda ve: ve.memset(ones_row, 1.0), inc=1)
    b.wait("t", [ev_oc, ev_or])

    w1q = sbuf(nc, ctx, "p2_w1q", [128, 8 * 4096], BF16)
    w2q = sbuf(nc, ctx, "p2_w2q", [128, 32 * 1024], BF16)
    with ExitStack() as qctx:
        ev_q = emit_quant(b, nc, qctx, io, w1q, w2q, ones_col, ones_row)
    if "dbg_w1q" in io:
        ev_q = b.dma("y", io["dbg_w1q"], w1q[:, 0:4096], w=[ev_q])
        ev_q = b.dma("y", io["dbg_w2q"], w2q[:, 0:4096], w=[ev_q])

    b1t = sbuf(nc, ctx, "p2_b1t", [128, 32], F32)
    b2rb = sbuf(nc, ctx, "p2_b2rb", [1, 1024], BF16)
    b2rf = sbuf(nc, ctx, "p2_b2rf", [1, 1024], F32)
    onesb = sbuf(nc, ctx, "p2_onesb", [1, 128], BF16)
    evb1 = b.dma("y", b1t, io["b1t"])
    evb2 = b.dma("y", b2rf, io["b2r"])
    evb2b = b.s(lambda sc: sc.copy(b2rb, b2rf), w=[evb2], inc=1)
    evob = b.v(lambda ve: ve.memset(onesb, 1.0), inc=1)

    NG = 8
    GT = 512
    xnt0 = sbuf(nc, ctx, "p2_xnt0", [128, 8 * GT], BF16)
    xnt = [xnt0, xnt0]
    xt0 = sbuf(nc, ctx, "p2_xt0", [128, 4 * 1024], F32)
    xt = [xt0, xt0]
    ht = sbuf(nc, ctx, "p2_ht", [128, 32 * GT], BF16)
    dbuf = sbuf(nc, ctx, "p2_dbuf", [128, 1024], F32)
    stat = sbuf(nc, ctx, "p2_stat", [128, 16], F32)
    x2t = sbuf(nc, ctx, "p2_x2t", [128, 1024], F32)
    xn2t = sbuf(nc, ctx, "p2_xn2t", [128, 1024], F32)
    xn2b = sbuf(nc, ctx, "p2_xn2b", [128, 1024], BF16)

    ps_h = [psum(nc, ctx, f"p2_psh{i}", [128, GT], F32) for i in range(2)]
    ps_o = [psum(nc, ctx, f"p2_pso{i}", [128, 512], F32) for i in range(2)]

    load_ev = {}

    def issue_loads(g, w, wx=()):
        par = g % 2
        ev = None
        for fic in range(8):
            ev = b.dma("y", xnt[par][:, fic * GT:(fic + 1) * GT],
                       io["xn_in"][g * GT:(g + 1) * GT, fic * 128:(fic + 1) * 128],
                       w=w, transpose=True)
        evx = None
        for tb in range(4):
            evx = b.dma("y", xt[par][:, tb * 1024:(tb + 1) * 1024],
                        io["x_in"][g * GT + tb * 128:g * GT + (tb + 1) * 128, :],
                        w=(wx if tb == 0 else ()))
        load_ev[g] = (ev, evx)

    issue_loads(0, [ev_q], wx=[ev_q])

    prev_sx = None
    prev_sxn = None
    consumed_xnt_prev = None
    group_done = {}

    for g in range(NG):
        par = g % 2
        ev_xnt, ev_xt = load_ev[g]
        h_done = []
        for foc in range(32):
            pb = foc % 2
            wv = [ev_xnt]
            if foc >= 2:
                wv.append(h_done[foc - 2])
            ev_mm = None
            for fic in range(8):
                ev_mm = b.t(lambda te, pb=pb, fic=fic, foc=foc, par=par: te.matmul(
                    ps_h[pb], w1q[:, fic * 4096 + foc * 128: fic * 4096 + (foc + 1) * 128],
                    xnt[par][:, fic * GT:(fic + 1) * GT],
                    start=(fic == 0), stop=(fic == 7)),
                    w=wv if fic == 0 else (), inc=(1 if fic == 7 else 0))
            wv_s = [ev_mm, evb1]
            ev_ev = b.s(lambda sc, pb=pb, foc=foc: sc.activation(
                ht[:, foc * GT:(foc + 1) * GT], ps_h[pb], AF.Relu,
                bias=b1t[:, foc:foc + 1], scale=1.0), w=wv_s, inc=1)
            h_done.append(ev_ev)
        if g == 0 and "dbg_xnt" in io:
            b.dma("y", io["dbg_xnt"], xnt[0][:, 0:4096], w=[ev_xnt])
            h_done[-1] = b.dma("y", io["dbg_ht"], ht[:, 0:1024], w=[h_done[0], h_done[1]])
        consumed_xnt = h_done[-1]

        prev_evac = []
        ev_x2 = None
        for tsub in range(4):
            ev_os = []
            for dh in range(2):
                wv = [h_done[-1], evb2b, evob] + prev_evac
                ev_o = None
                for foc in range(32):
                    b.t(lambda te, dh=dh, foc=foc, tsub=tsub: te.matmul(
                        ps_o[dh], ht[:, foc * GT + tsub * 128: foc * GT + tsub * 128 + 128],
                        w2q[:, foc * 1024 + dh * 512: foc * 1024 + (dh + 1) * 512],
                        start=(foc == 0), stop=False),
                        w=wv if foc == 0 else ())
                ev_o = b.t(lambda te, dh=dh: te.matmul(
                    ps_o[dh], onesb, b2rb[0:1, dh * 512:(dh + 1) * 512],
                    start=False, stop=True), inc=1)
                ev_os.append(ev_o)
            wv_d = [ev_os[1]]
            if prev_sx is not None:
                wv_d.append(prev_sx)
            if prev_sxn is not None:
                wv_d.append(prev_sxn)
            ev_d0 = b.v(lambda ve: ve.tensor_copy(dbuf[:, 0:512], ps_o[0]), w=wv_d, inc=1)
            ev_d1 = b.v(lambda ve: ve.tensor_copy(dbuf[:, 512:1024], ps_o[1]), inc=1)
            prev_evac = [ev_d0, ev_d1]
            if g == 0 and tsub == 0 and "dbg_dbuf" in io:
                prev_evac.append(b.dma("y", io["dbg_dbuf"], dbuf, w=[ev_d1]))
            a_c, b_c, ev_st = ln.stats(dbuf, x2t, stat, 0, D, w=[ev_d1])
            if g == 0 and tsub == 0 and "dbg_stat" in io:
                b.dma("y", io["dbg_stat"], stat, w=[ev_st])
            ev_ap = b.v(lambda ve: ve.tensor_scalar(
                x2t, dbuf, a_c, b_c, op0=ALU.mult, op1=ALU.add), w=[ev_st])
            ev_x2 = b.v(lambda ve, par=par, tsub=tsub: ve.tensor_tensor(
                x2t, x2t, xt[par][:, tsub * 1024:(tsub + 1) * 1024], op=ALU.add),
                w=[ev_xt], inc=1)
            prev_sx = b.dma("y", io["x_out"][g * GT + tsub * 128: g * GT + (tsub + 1) * 128, :],
                            x2t, w=[ev_x2])
            a2, b2_, ev_st2 = ln.stats(x2t, xn2t, stat, 8, D, w=[ev_x2])
            ev_ap2 = b.v(lambda ve: ve.tensor_scalar(
                xn2t, x2t, a2, b2_, op0=ALU.mult, op1=ALU.add), w=[ev_st2], inc=1)
            wv_cb2 = [ev_ap2] + ([prev_sxn] if prev_sxn is not None else [])
            ev_cb2 = b.v(lambda ve: ve.tensor_copy(xn2b, xn2t), w=wv_cb2, inc=1)
            prev_sxn = b.dma("y", io["xn_out"][g * GT + tsub * 128: g * GT + (tsub + 1) * 128, :],
                             xn2b, w=[ev_cb2])
        group_done[g] = (consumed_xnt, ev_x2)
        if g + 1 < NG:
            issue_loads(g + 1, [consumed_xnt], wx=[ev_x2])

    b.wait("y", [("y", b.sems["y"].n)])
    b.wait("g", [("gd", b.sems["gd"].n)])


def emit_p1(b, nc, ctx, io):
    ln = LNHelper(b)
    xn0buf = io["xn0buf"]
    y2buf = io["y2buf"]
    r_dram = io["r_dram"]
    lamp_df = io["lamp_df"]
    lamp_db = io["lamp_db"]
    l128_d = io["l128_d"]

    wq_s = sbuf(nc, ctx, "p1_wq", [128, 8 * 1024], BF16)
    wv_s = sbuf(nc, ctx, "p1_wv", [128, 8 * 1024], BF16)
    wg_s = sbuf(nc, ctx, "p1_wg", [128, 8 * 1024], BF16)
    wo_s = sbuf(nc, ctx, "p1_wo", [128, 8 * 1024], BF16)
    g0f = sbuf(nc, ctx, "p1_g0", [128, 1024], F32)
    b0f = sbuf(nc, ctx, "p1_b0", [128, 1024], F32)
    lt_s = sbuf(nc, ctx, "p1_lt", [128, 16 * 128], BF16)
    lam_s = sbuf(nc, ctx, "p1_lam", [128, 16], BF16)
    lamrow = sbuf(nc, ctx, "p1_lamrow", [1, 2048], BF16)
    l128row = sbuf(nc, ctx, "p1_l128row", [1, 2048], F32)
    carry = sbuf(nc, ctx, "p1_carry", [1, 2048], F32)
    cin_bf = sbuf(nc, ctx, "p1_cin", [1, 4096], BF16)
    t1row = sbuf(nc, ctx, "p1_t1row", [1, 2048], F32)
    rrow = sbuf(nc, ctx, "p1_rrow", [1, 4096], F32)
    r_sb = sbuf(nc, ctx, "p1_rsb", [16, 512], F32)
    onesr = sbuf(nc, ctx, "p1_onesr", [1, 128], F32)
    stat = sbuf(nc, ctx, "p1_stat", [128, 16], F32)
    sq = sbuf(nc, ctx, "p1_sq", [128, 1024], F32)

    ev_w = None
    for dst, key in ((wq_s, "w0qT"), (wv_s, "w0vT"), (wg_s, "w0gT"), (wo_s, "w0oT")):
        for fic in range(8):
            ev_w = b.dma("g", dst[:, fic * 1024:(fic + 1) * 1024],
                         io[key][fic * 128:(fic + 1) * 128, :])
    ev_g0 = b.dma("y", g0f, io["g0full"])
    ev_b0 = b.dma("y", b0f, io["b0full"])

    sctx = ExitStack()
    dec_c = sbuf(nc, sctx, "p1_dec", [16, 1], F32)
    lncol = sbuf(nc, sctx, "p1_lncol", [16, 1], F32)
    lnrow = sbuf(nc, sctx, "p1_lnrow", [1, 16], F32)
    lnb = sbuf(nc, sctx, "p1_lnb", [128, 16], F32)
    dmat = sbuf(nc, sctx, "p1_dmat", [128, 128], F32)
    drelu = sbuf(nc, sctx, "p1_drelu", [128, 128], F32)
    uptri = sbuf(nc, sctx, "p1_uptri", [128, 128], F32)
    ltf = sbuf(nc, sctx, "p1_ltf", [128, 128], F32)
    jtab = sbuf(nc, sctx, "p1_jtab", [16, 130], F32)
    lamp = sbuf(nc, sctx, "p1_lamp", [16, 130], F32)
    i127 = sbuf(nc, sctx, "p1_i127", [128, 1], F32)
    l128r16 = sbuf(nc, sctx, "p1_l128r16", [1, 16], F32)
    l128t = sbuf(nc, sctx, "p1_l128t", [16, 128], F32)
    ps_bc = psum(nc, sctx, "p1_psbc", [128, 16], F32)
    ps_b2 = psum(nc, sctx, "p1_psb2", [16, 128], F32)

    e_or = b.v(lambda ve: ve.memset(onesr, 1.0), inc=1)
    e_cz = b.v(lambda ve: ve.memset(carry, 0.0), inc=1)
    e_dc = b.dma("y", dec_c, io["decay_col"])
    e_dr = b.dma("y", lnrow, io["decay_row"])
    e_ut = b.dma("y", uptri, io["c_uptri"])
    e1 = b.s(lambda sc: sc.activation(lncol, dec_c, AF.Ln), w=[e_dc], inc=1)
    e2 = b.s(lambda sc: sc.activation(lnrow, lnrow, AF.Ln), w=[e_dr], inc=1)
    e3 = b.t(lambda te: te.matmul(ps_bc, onesr, lnrow, start=True, stop=True),
             w=[e_or, e2], inc=1)
    e4 = b.s(lambda sc: sc.copy(lnb, ps_bc), w=[e3], inc=1)
    e5 = b.g(lambda gp: gp.iota(dmat, pattern=[[1, 128]], base=0, channel_multiplier=-1,
                                allow_small_or_imprecise_dtypes=True), inc=1)
    e6 = b.v(lambda ve: ve.tensor_scalar(drelu, dmat, 0.0, None, op0=ALU.max), w=[e5], inc=1)
    e_lt = None
    for h in range(16):
        ea = b.s(lambda sc, h=h: sc.activation(ltf, drelu, AF.Exp, scale=lnb[:, h:h + 1]),
                 w=[e6, e4] + ([e_lt] if e_lt else []), inc=1)
        e_lt = b.v(lambda ve, h=h: ve.tensor_tensor(
            lt_s[:, h * 128:(h + 1) * 128], ltf, uptri, op=ALU.mult), w=[ea, e_ut], inc=1)
    e7 = b.g(lambda gp: gp.iota(i127, pattern=[[1, 1]], base=127, channel_multiplier=-1,
                                allow_small_or_imprecise_dtypes=True), inc=1)
    e_lam = None
    for h in range(16):
        e_lam = b.s(lambda sc, h=h: sc.activation(
            lam_s[:, h:h + 1], i127, AF.Exp, scale=lnb[:, h:h + 1]), w=[e7, e4], inc=1)
    e8 = b.g(lambda gp: gp.iota(jtab, pattern=[[1, 130]], base=0, channel_multiplier=0,
                                allow_small_or_imprecise_dtypes=True), inc=1)
    e9 = b.s(lambda sc: sc.activation(lamp, jtab, AF.Exp, scale=lncol), w=[e8, e1], inc=1)
    lampb = sbuf(nc, sctx, "p1_lampb", [16, 130], BF16)
    e10f = b.dma("y", lamp_df, lamp, w=[e9])
    e10bc = b.s(lambda sc: sc.copy(lampb, lamp), w=[e9], inc=1)
    e10b = b.dma("y", lamp_db, lampb, w=[e10bc])
    e11 = b.dma("y", lamrow, lamp_db[0:16, 1:129], w=[e10b])
    e12 = b.dma("y", l128r16, lamp_df[0:16, 128:129], w=[e10f], slow=True)
    e13 = b.t(lambda te: te.matmul(ps_b2, l128r16, onesr, start=True, stop=True),
              w=[e12, e_or], inc=1)
    e14 = b.s(lambda sc: sc.copy(l128t, ps_b2), w=[e13], inc=1)
    e15 = b.dma("y", l128_d, l128t, w=[e14])
    e16 = b.dma("y", l128row, l128_d[0:16, :], w=[e15])
    setup_done = [e_lt, e_lam, e10b, e16, e11]
    sctx.close()

    actx = ExitStack()
    xa = [sbuf(nc, actx, f"p1a_x{i}", [128, 1024], F32) for i in range(2)]
    xna = [sbuf(nc, actx, f"p1a_xn{i}", [128, 1024], F32) for i in range(2)]
    xnab = [sbuf(nc, actx, f"p1a_xnb{i}", [128, 1024], BF16) for i in range(2)]
    p1a_done = []
    prev_store = [None, None]
    for tt in range(NTB):
        par = tt % 2
        wv = [p1a_done[tt - 2][1]] if tt >= 2 else list(setup_done)
        evl = b.dma("y", xa[par], io["x_in"][tt * 128:(tt + 1) * 128, :], w=wv)
        a_c, b_c, ev_st = ln.stats(xa[par], xna[par], stat, 0, D,
                                   w=[evl] + ([prev_store[par]] if prev_store[par] else []))
        ev_ap = b.v(lambda ve, par=par: ve.tensor_scalar(
            xna[par], xa[par], a_c, b_c, op0=ALU.mult, op1=ALU.add), w=[ev_st], inc=1)
        ev_cb = b.v(lambda ve, par=par: ve.tensor_copy(xnab[par], xna[par]), w=[ev_ap], inc=1)
        ev_s = b.dma("y", xn0buf[tt * 128:(tt + 1) * 128, :], xnab[par], w=[ev_cb])
        prev_store[par] = ev_s
        p1a_done.append((ev_ap, ev_s))

    p1a_all = ("gd", b.sems["gd"].n)
    p1a_all_v = ("v", b.sems["v"].n)
    actx.close()

    xnt = sbuf(nc, ctx, "p1_xnt", [128, 8 * 512], BF16)
    xt = sbuf(nc, ctx, "p1_xt", [128, 1024], F32)
    q_s = sbuf(nc, ctx, "p1_qs", [128, 4 * 1024], F32)
    v_s = sbuf(nc, ctx, "p1_vs", [128, 4 * 1024], BF16)
    gate_s = sbuf(nc, ctx, "p1_gate", [128, 4 * 1024], F32)
    y2b = sbuf(nc, ctx, "p1_y2b", [128, 1024], BF16)
    y2t = sbuf(nc, ctx, "p1_y2t", [128, 8 * 128], BF16)
    obuf = sbuf(nc, ctx, "p1_obuf", [128, 1024], F32)
    x1t = sbuf(nc, ctx, "p1_x1t", [128, 1024], F32)
    xn1t = sbuf(nc, ctx, "p1_xn1t", [128, 1024], F32)
    xn1b = sbuf(nc, ctx, "p1_xn1b", [128, 1024], BF16)

    pq = psum(nc, ctx, "p1_pq", [128, 1024], F32)
    pv = psum(nc, ctx, "p1_pv", [128, 1024], F32)
    pg = psum(nc, ctx, "p1_pg", [128, 1024], F32)
    ps_st = psum(nc, ctx, "p1_psst", [128, 256], F32)
    ps_r = psum(nc, ctx, "p1_psr", [16, 512], F32)

    def strided4(t, h, width=1024):
        return t.rearrange("p (blk f) -> p blk f", blk=4)[:, :, h * 64:(h + 1) * 64]

    NS = 8
    prev = {}
    for st in range(NS):
        rows = [st * 256, st * 256 + 128, 2048 + st * 256, 2048 + st * 256 + 128]
        wv_xn = [prev["xnt_free"]] if "xnt_free" in prev else [ev_w, p1a_all, p1a_all_v]
        need = []
        ev_xnt = None
        for seq in range(2):
            r0 = rows[seq * 2]
            for fic in range(8):
                ev_xnt = b.dma("y", xnt[:, fic * 512 + seq * 256: fic * 512 + seq * 256 + 256],
                               xn0buf[r0:r0 + 256, fic * 128:(fic + 1) * 128],
                               w=(wv_xn + need) if (seq == 0 and fic == 0) else (),
                               transpose=True)
        ev_evq = ev_evv = ev_evg = None
        sil_evs = []
        setup_ps_waits = [e4, e14] if st == 0 else []
        for blk in range(4):
            seq, ch = blk // 2, blk % 2
            wv0 = [ev_xnt] + setup_ps_waits
            if "q_free" in prev and blk == 0:
                wv0 += [prev["q_free"], prev["v_free"]]
            if blk > 0:
                wv0 = [ev_evg, ev_evv]
            ev_mm = None
            for fic in range(8):
                lhs = xnt[:, fic * 512 + seq * 256 + ch * 128: fic * 512 + seq * 256 + ch * 128 + 128]
                for wi, (ws, ps) in enumerate(((wq_s, pq), (wv_s, pv), (wg_s, pg))):
                    for half in range(2):
                        ev_mm = b.t(lambda te, lhs=lhs, ws=ws, ps=ps, fic=fic, half=half: te.matmul(
                            ps[:, half * 512:(half + 1) * 512], lhs,
                            ws[:, fic * 1024 + half * 512: fic * 1024 + (half + 1) * 512],
                            start=(fic == 0), stop=(fic == 7)),
                            w=wv0 if (fic == 0 and wi == 0 and half == 0) else (),
                            inc=(1 if (fic == 7 and wi == 2 and half == 1) else 0))
            ev_evq = b.v(lambda ve, blk=blk: ve.tensor_copy(
                q_s[:, blk * 1024:(blk + 1) * 1024], pq), w=[ev_mm], inc=1)
            ev_evv = b.v(lambda ve, blk=blk: ve.tensor_copy(
                v_s[:, blk * 1024:(blk + 1) * 1024], pv), inc=1)
            wv_sil = [ev_mm]
            if blk == 0 and "g_free" in prev:
                wv_sil.append(prev["g_free"])
            ev_evg = b.s(lambda sc, blk=blk: sc.activation(
                gate_s[:, blk * 1024:(blk + 1) * 1024], pg, AF.Silu), w=wv_sil, inc=1)
            sil_evs.append(ev_evg)
        ev_vs_done = ev_evv

        ev_rc = None
        for k in range(8):
            ev_rm = b.t(lambda te, k=k: te.matmul(
                ps_r, lam_s, v_s[:, k * 512:(k + 1) * 512], start=True, stop=True),
                w=[ev_vs_done, e_lam] if k == 0 else ([ev_rc] if ev_rc else ()), inc=1)
            wv_rc = [ev_rm]
            if "rsb_free" in prev:
                wv_rc.append(prev["rsb_free"])
            ev_rc = b.s(lambda sc: sc.copy(r_sb, ps_r), w=wv_rc, inc=1)
            wv_rs = [ev_rc]
            if k == 0 and "rdram_free" in prev:
                wv_rs.append(prev["rdram_free"])
            ev_rst = b.dma("y", r_dram[:, k * 512:(k + 1) * 512], r_sb, w=wv_rs)
            prev["rsb_free"] = ev_rst
        ev_rl = None
        for h in range(16):
            src_v = r_dram.rearrange("h (blk f) -> h blk f", blk=4)[h:h + 1, :, h * 64:(h + 1) * 64]
            dst_v = rrow[0:1, h * 256:(h + 1) * 256].rearrange("o (blk f) -> o blk f", blk=4)
            wv_rl = [ev_rst] if h == 0 else ()
            if h == 0 and "rrow_free" in prev:
                wv_rl = [ev_rst, prev["rrow_free"]]
            ev_rl = b.dma("y", dst_v, src_v, w=wv_rl)
        prev["rdram_free"] = ev_rl
        r4 = rrow.rearrange("o (h blk f) -> o h blk f", h=16, blk=4)
        ci4 = cin_bf.rearrange("o (h blk f) -> o h blk f", h=16, blk=4)
        c2 = carry.rearrange("o (h s f) -> o h s f", h=16, s=2)
        t2 = t1row.rearrange("o (h s f) -> o h s f", h=16, s=2)
        e = b.v(lambda ve: ve.tensor_copy(ci4[:, :, 0::2, :], c2), w=[ev_rl, e11, e16] + ([prev["cin_free"]] if "cin_free" in prev else []), inc=1)
        e = b.v(lambda ve: ve.tensor_tensor(t1row, carry, l128row, op=ALU.mult), w=[e], inc=1)
        e = b.v(lambda ve: ve.tensor_tensor(t2, t2, r4[:, :, 0::2, :], op=ALU.add), w=[e], inc=1)
        e = b.v(lambda ve: ve.tensor_copy(ci4[:, :, 1::2, :], t2), w=[e], inc=1)
        e = b.v(lambda ve: ve.tensor_tensor(carry, t1row, l128row, op=ALU.mult), w=[e], inc=1)
        ev_chain = b.v(lambda ve: ve.tensor_tensor(c2, c2, r4[:, :, 1::2, :], op=ALU.add), w=[e], inc=1)
        prev["rrow_free"] = ev_chain

        ev_y = None
        ev_int = None
        for h in range(16):
            wv0 = [ev_chain] if h == 0 else ([ev_y])
            b.t(lambda te, h=h: te.matmul(
                ps_st, lt_s[:, h * 128:(h + 1) * 128], strided4(v_s, h),
                start=True, stop=False), w=wv0)
            ev_int = b.t(lambda te, h=h: te.matmul(
                ps_st, lamrow[0:1, h * 128:(h + 1) * 128], cin_bf[0:1, h * 256:(h + 1) * 256],
                start=False, stop=True), inc=1)
            ev_y = b.v(lambda ve, h=h: ve.tensor_tensor(
                strided4(q_s, h), ps_st.rearrange("p (blk f) -> p blk f", blk=4),
                strided4(q_s, h), op=ALU.mult), w=[ev_int], inc=1)
        prev["cin_free"] = ev_y
        prev["q_free"] = ev_y
        prev["xnt_free"] = ev_int

        ev_x1_last = None
        for blk in range(4):
            wv_xt = [prev["xt_free"]] if "xt_free" in prev else []
            ev_xt = b.dma("y", xt, io["x_in"][rows[blk]:rows[blk] + 128, :], w=wv_xt)
            prev["xt_free"] = None
            ys = q_s[:, blk * 1024:(blk + 1) * 1024]
            a_c, b_c, ev_st2 = ln.stats(ys, sq, stat, 0, D, w=[ev_y])
            e = b.v(lambda ve, ys=ys: ve.tensor_scalar(ys, ys, a_c, b_c, op0=ALU.mult, op1=ALU.add), w=[ev_st2], inc=1)
            e = b.v(lambda ve, ys=ys: ve.tensor_tensor(ys, ys, g0f, op=ALU.mult), w=[e, ev_g0], inc=1)
            e = b.v(lambda ve, ys=ys: ve.tensor_tensor(ys, ys, b0f, op=ALU.add), w=[e, ev_b0], inc=1)
            wv_y2 = [e, sil_evs[blk]] + ([prev["y2b_free"]] if "y2b_free" in prev else [])
            ev_y2 = b.v(lambda ve, ys=ys, blk=blk: ve.tensor_tensor(
                y2b, ys, gate_s[:, blk * 1024:(blk + 1) * 1024], op=ALU.mult), w=wv_y2, inc=1)
            r0 = rows[blk]
            ev_y2s = b.dma("y", y2buf[r0:r0 + 128, :], y2b, w=[ev_y2])
            prev["y2b_free"] = ev_y2s
            ev_y2t = None
            for fic in range(8):
                ev_y2t = b.dma("y", y2t[:, fic * 128:(fic + 1) * 128],
                               y2buf[r0:r0 + 128, fic * 128:(fic + 1) * 128],
                               w=[ev_y2s, prev.get("y2t_free") or ev_y2s] if fic == 0 else (),
                               transpose=True)
            pq_free = prev.get("pq_free") or ev_evq
            ev_wo = None
            for fic in range(8):
                for half in range(2):
                    ev_wo = b.t(lambda te, fic=fic, half=half: te.matmul(
                        pq[:, half * 512:(half + 1) * 512], y2t[:, fic * 128:(fic + 1) * 128],
                        wo_s[:, fic * 1024 + half * 512: fic * 1024 + (half + 1) * 512],
                        start=(fic == 0), stop=(fic == 7)),
                        w=[ev_y2t, pq_free] if (fic == 0 and half == 0) else (),
                        inc=(1 if (fic == 7 and half == 1) else 0))
            prev["y2t_free"] = ev_wo
            ev_ob = b.v(lambda ve: ve.tensor_copy(obuf, pq), w=[ev_wo], inc=1)
            prev["pq_free"] = ev_ob
            wv_st3 = [ev_ob] + ([prev["x1t_free"]] if "x1t_free" in prev else [])
            a2, b2c, ev_st3 = ln.stats(obuf, x1t, stat, 8, D, w=wv_st3)
            e = b.v(lambda ve: ve.tensor_scalar(x1t, obuf, a2, b2c, op0=ALU.mult, op1=ALU.add), w=[ev_st3], inc=1)
            ev_x1 = b.v(lambda ve: ve.tensor_tensor(
                x1t, x1t, xt, op=ALU.add), w=[e, ev_xt], inc=1)
            ev_sx = b.dma("y", io["x_out"][r0:r0 + 128, :], x1t, w=[ev_x1])
            wv_st4 = [ev_x1] + ([prev["xn1t_free"]] if "xn1t_free" in prev else [])
            a3, b3c, ev_st4 = ln.stats(x1t, xn1t, stat, 0, D, w=wv_st4)
            ev_ap3 = b.v(lambda ve: ve.tensor_scalar(
                xn1t, x1t, a3, b3c, op0=ALU.mult, op1=ALU.add), w=[ev_st4], inc=1)
            wv_cb = [ev_ap3] + ([prev["xn1b_free"]] if "xn1b_free" in prev else [])
            ev_cb3 = b.v(lambda ve: ve.tensor_copy(xn1b, xn1t), w=wv_cb, inc=1)
            ev_sxn = b.dma("y", io["xn_out"][r0:r0 + 128, :], xn1b, w=[ev_cb3])
            prev["xn1b_free"] = ev_sxn
            ev_x1_last = ev_sx
            prev["x1t_free"] = ev_sx
            prev["xn1t_free"] = ev_sxn
            prev["xt_free"] = ev_x1
        prev["q_free"] = prev["pq_free"]
        prev["v_free"] = ev_evv
        prev["g_free"] = ("v", b.sems["v"].n)

    b.wait("y", [("y", b.sems["y"].n)])
    b.wait("g", [("gd", b.sems["gd"].n)])


def emit_p4(b, nc, ctx, io):
    ln = LNHelper(b)
    wbd_d = io["wbd_d"]
    wcb_d = io["wcb_d"]

    ev_wc = None
    for i in range(16):
        ev_wc = b.dma("g", wcb_d[i * 1024:(i + 1) * 1024, :],
                      io["wcT"][i * 1024:(i + 1) * 1024, :])
    with ExitStack() as sctx:
        rm_s = sbuf(nc, sctx, "p4_rm", [16, 4096], BF16)
        ru_s = sbuf(nc, sctx, "p4_ru", [16, 16384], BF16)
        cstage = sbuf(nc, sctx, "p4_cst", [16, 64], BF16)
        zt = sbuf(nc, sctx, "p4_zt", [128, 512], BF16)
        ps_c = [psum(nc, sctx, f"p4_psc{i}", [16, 64], F32) for i in range(2)]
        ev_rm = b.dma("g", rm_s, io["railmJ"])
        ev_ru = b.dma("g", ru_s, io["railuJ"])
        ev_z = b.v(lambda ve: ve.memset(zt, 0.0), inc=1)
        ev_zs = None
        for i in range(32):
            ev_zs = b.dma("y", wbd_d[i * 128:(i + 1) * 128, :], zt, w=[ev_z] if i == 0 else ())
        ev_cp = None
        ev_cs = None
        for h in range(4):
            for r in range(64):
                pi = r % 2
                wv = [ev_rm, ev_ru] if (h == 0 and r == 0) else ([ev_cp] if r >= 2 or h > 0 else [])
                ev_mmc = b.t(lambda te, h=h, r=r, pi=pi: te.matmul(
                    ps_c[pi], rm_s[:, (h * 64 + r) * 16:(h * 64 + r + 1) * 16],
                    ru_s[:, (h * 64 + r) * 64:(h * 64 + r + 1) * 64],
                    start=True, stop=True), w=wv, inc=1)
                wv_c = [ev_mmc] + ([ev_cs] if ev_cs else [ev_zs])
                ev_cp = b.s(lambda sc, pi=pi: sc.copy(cstage, ps_c[pi]), w=wv_c, inc=1)
                g, rl = r // 8, r % 8
                ev_cs = b.dma("y", wbd_d[(h * 8 + g) * 128 + rl * 16:(h * 8 + g) * 128 + rl * 16 + 16,
                                         rl * 64:(rl + 1) * 64], cstage, w=[ev_cp])
        ev_bd_done = ev_cs

    wbd_s = sbuf(nc, ctx, "p4_wbd", [128, 32 * 512], BF16)
    ht = sbuf(nc, ctx, "p4_ht", [128, 128 * 512], BF16)
    xnt = sbuf(nc, ctx, "p4_xnt", [128, 8 * 512], BF16)
    wcs = [sbuf(nc, ctx, f"p4_wcs{i}", [128, 2048], BF16) for i in range(2)]
    xt = sbuf(nc, ctx, "p4_xt", [128, 1024], F32)
    dbuf = sbuf(nc, ctx, "p4_dbuf", [128, 4 * 1024], F32)
    x2t = sbuf(nc, ctx, "p4_x2t", [128, 1024], F32)
    stat = sbuf(nc, ctx, "p4_stat", [128, 16], F32)
    ps_z = [psum(nc, ctx, f"p4_psz{i}", [128, 512], F32) for i in range(2)]
    ps_o = [psum(nc, ctx, f"p4_pso{i}", [128, 512], F32) for i in range(4)]

    ev_bd = None
    for i in range(32):
        ev_bd = b.dma("y", wbd_s[:, i * 512:(i + 1) * 512],
                      wbd_d[i * 128:(i + 1) * 128, :], w=[ev_bd_done] if i == 0 else ())
    if "dbg_wbd" in io:
        b.dma("y", io["dbg_wbd"], wbd_s[:, 0:512], w=[ev_bd])

    NG = 8
    GT = 512
    prev = {}
    for g in range(NG):
        wv_xn = [prev["xnt_free"]] if "xnt_free" in prev else []
        ev_xnt = None
        for fic in range(8):
            ev_xnt = b.dma("y", xnt[:, fic * GT:(fic + 1) * GT],
                           io["xn_in"][g * GT:(g + 1) * GT, fic * 128:(fic + 1) * 128],
                           w=wv_xn if fic == 0 else (), transpose=True)
        h_done = None
        ci = 0
        for rg in range(8):
            for hh in range(4):
                for mc in range(4):
                    pi = ci % 2
                    wv = [ev_xnt, ev_bd] if ci == 0 else []
                    if "ht_free" in prev and ci == 0:
                        wv.append(prev["ht_free"])
                    if ci >= 2:
                        wv = [h_evs[ci - 2]]
                    tile_off = (hh * 8 + rg) * 512
                    ev_mm = b.t(lambda te, pi=pi, tile_off=tile_off, mc=mc, rg=rg: te.matmul(
                        ps_z[pi], wbd_s[:, tile_off + mc * 128: tile_off + mc * 128 + 128],
                        xnt[:, rg * GT:(rg + 1) * GT],
                        start=True, stop=True), w=wv, inc=1)
                    cc = hh * 32 + rg * 4 + mc
                    ev_h = b.s(lambda sc, pi=pi, cc=cc: sc.activation(
                        ht[:, cc * GT:(cc + 1) * GT], ps_z[pi], AF.Gelu), w=[ev_mm], inc=1)
                    if ci == 0:
                        h_evs = {}
                    h_evs[ci] = ev_h
                    ci += 1
        h_done = h_evs[ci - 1]
        prev["xnt_free"] = h_done
        if g == 0 and "dbg_ht" in io:
            h_done = b.dma("y", io["dbg_ht"], ht[:, 0:1024], w=[h_done])
            h_done = b.dma("y", io["dbg_ht2"][:, 0:512], ht[:, 4 * 512:5 * 512], w=[h_done])
            h_done = b.dma("y", io["dbg_ht2"][:, 512:1024], ht[:, 32 * 512:33 * 512], w=[h_done])
            b.dma("y", io["dbg_xnt4"], xnt[:, 0:4096], w=[h_done])

        ev_ep_done = []
        for dh in range(2):
            ev_o = [None] * 4
            ev_wcl = {}

            def wc_src(pp):
                return wcb_d[pp * 256:(pp + 1) * 256, :].rearrange("(a p) n -> p a n", a=2)

            wv0 = [prev["wc_free"]] if "wc_free" in prev else [ev_wc]
            ev_wcl[0] = b.dma("y", wcs[0].rearrange("p (a n) -> p a n", a=2), wc_src(0), w=wv0)
            foc_done = {}
            for foc in range(128):
                pp = foc // 2
                par = pp % 2
                if foc % 2 == 0 and pp + 1 < 64:
                    wv_l = [foc_done[2 * pp - 1]] if pp >= 1 else []
                    ev_wcl[pp + 1] = b.dma("y", wcs[(pp + 1) % 2].rearrange("p (a n) -> p a n", a=2),
                                           wc_src(pp + 1), w=wv_l)
                for tsub in range(4):
                    wv = []
                    if foc == 0 and tsub == 0:
                        wv = [h_done, ev_wcl[0]] + ([prev["pso_free"]] if "pso_free" in prev else [])
                    elif tsub == 0 and foc % 2 == 0:
                        wv = [ev_wcl[pp]]
                    ev_o[tsub] = b.t(lambda te, par=par, foc=foc, tsub=tsub, dh=dh: te.matmul(
                        ps_o[tsub], ht[:, foc * GT + tsub * 128: foc * GT + tsub * 128 + 128],
                        wcs[par][:, (foc % 2) * 1024 + dh * 512: (foc % 2) * 1024 + dh * 512 + 512],
                        start=(foc == 0), stop=(foc == 127)),
                        w=wv, inc=(1 if tsub == 3 else 0))
                foc_done[foc] = ev_o[3]
            prev["wc_free"] = ("t", b.sems["t"].n)
            for tsub in range(4):
                db = dbuf[:, tsub * 1024 + dh * 512: tsub * 1024 + (dh + 1) * 512]
                wv_d = [ev_o[3]]
                if ("dbuf_free", tsub) in prev:
                    wv_d.append(prev.pop(("dbuf_free", tsub)))
                ev_d = b.v(lambda ve, db=db, tsub=tsub: ve.tensor_copy(db, ps_o[tsub]), w=wv_d, inc=1)
                if dh == 0:
                    prev[("dbuf0", tsub)] = ev_d
                else:
                    dfull = dbuf[:, tsub * 1024:(tsub + 1) * 1024]
                    wv_xt = [prev["xt_free"]] if "xt_free" in prev else []
                    ev_xt = b.dma("y", xt, io["x_in"][g * GT + tsub * 128: g * GT + (tsub + 1) * 128, :], w=wv_xt)
                    wv_st = [ev_d, prev[("dbuf0", tsub)]]
                    if "x2t_free" in prev:
                        wv_st.append(prev["x2t_free"])
                    a_c, b_c, ev_st = ln.stats(dfull, x2t, stat, 0, D, w=wv_st)
                    e = b.v(lambda ve, dfull=dfull: ve.tensor_scalar(
                        x2t, dfull, a_c, b_c, op0=ALU.mult, op1=ALU.add), w=[ev_st], inc=1)
                    if g == 0 and tsub == 0 and "dbg_dbuf" in io:
                        ev_st = b.dma("y", io["dbg_dbuf"], dfull, w=[ev_st])
                        b.wait("v", [ev_st])
                    ev_x2 = b.v(lambda ve: ve.tensor_tensor(x2t, x2t, xt, op=ALU.add), w=[e, ev_xt], inc=1)
                    ev_sx = b.dma("y", io["x_out"][g * GT + tsub * 128: g * GT + (tsub + 1) * 128, :],
                                  x2t, w=[ev_x2])
                    prev["x2t_free"] = ev_sx
                    prev["xt_free"] = ev_x2
                    prev[("dbuf_free", tsub)] = ev_x2
                    ev_ep_done.append(ev_sx)
            if dh == 0:
                prev["pso_free"] = prev[("dbuf0", 3)]
            else:
                prev["pso_free"] = ("v", b.sems["v"].n)
        prev["ht_free"] = ("t", b.sems["t"].n)

    b.wait("y", [("y", b.sems["y"].n)])
    b.wait("g", [("gd", b.sems["gd"].n)])


def build_p4_program():
    nc = bass.Bass("TRN2", target_bir_lowering=False)
    settle_scr = nc.dram_tensor("settle_scr", [8, 128], F32, kind="Internal").ap()
    io = {}
    io["x_in"] = nc.dram_tensor("x_in", [NT, D], F32, kind="ExternalInput").ap()
    io["xn_in"] = nc.dram_tensor("xn_in", [NT, D], BF16, kind="ExternalInput").ap()
    io["railmJ"] = nc.dram_tensor("railmJ", [16, 4096], F32, kind="ExternalInput").ap()
    io["railuJ"] = nc.dram_tensor("railuJ", [16, 16384], F32, kind="ExternalInput").ap()
    io["wcT"] = nc.dram_tensor("wcT", [DFF1, D], F32, kind="ExternalInput").ap()
    io["x_out"] = nc.dram_tensor("x_out", [NT, D], F32, kind="ExternalOutput").ap()
    io["wbd_d"] = nc.dram_tensor("wbd_d", [4096, 512], BF16, kind="Internal").ap()
    io["wcb_d"] = nc.dram_tensor("wcb_d", [DFF1, D], BF16, kind="Internal").ap()
    with ExitStack() as ctx:
        b = B(nc, ctx, sem_offset=24)
        rng = _pre_clean(nc, b, settle_scr=settle_scr)
        emit_p4(b, nc, ctx, io)
        blk = ctx.enter_context(nc.Block())
        b.finalize(blk)
    _post_clean(nc, rng)
    return nc


def emit_p3(b, nc, ctx, io):
    ln = LNHelper(b, expln=True)
    atbuf = io["atbuf"]

    wq_s = sbuf(nc, ctx, "p3_wq", [128, 8 * 1024], BF16)
    wk_s = sbuf(nc, ctx, "p3_wk", [128, 8 * 1024], BF16)
    wv_s = sbuf(nc, ctx, "p3_wv", [128, 8 * 1024], BF16)
    wo_s = sbuf(nc, ctx, "p3_wo", [128, 8 * 1024], BF16)
    canary = sbuf(nc, ctx, "p3_canary", [128, 256], F32) if "dbg_canary" in io else None
    k_s = sbuf(nc, ctx, "p3_ks", [128, 8 * 2048], BF16)
    v_s = sbuf(nc, ctx, "p3_vs", [128, 16 * 1040], BF16)
    q_t = sbuf(nc, ctx, "p3_qt", [128, 8 * 512], BF16)
    p_s = sbuf(nc, ctx, "p3_ps", [128, 2048], BF16)
    ptb = sbuf(nc, ctx, "p3_ptb", [128, 16 * 128], BF16)
    attn_tok = sbuf(nc, ctx, "p3_attn", [128, 1024], BF16)
    attnT = sbuf(nc, ctx, "p3_attnT", [128, 8 * 128], BF16)
    xnt = sbuf(nc, ctx, "p3_xnt", [128, 8 * 512], BF16)
    xt = sbuf(nc, ctx, "p3_xt", [128, 1024], F32)
    obuf = sbuf(nc, ctx, "p3_obuf", [128, 1024], F32)
    x3t = sbuf(nc, ctx, "p3_x3t", [128, 1024], F32)
    xn3t = sbuf(nc, ctx, "p3_xn3t", [128, 1024], F32)
    xn3b = sbuf(nc, ctx, "p3_xn3b", [128, 1024], BF16)
    stat = sbuf(nc, ctx, "p3_stat", [128, 16], F32)
    mstat = sbuf(nc, ctx, "p3_mstat", [128, 8], F32)
    zc = sbuf(nc, ctx, "p3_zc", [128, 2], F32)
    identb = sbuf(nc, ctx, "p3_ident", [128, 128], BF16)
    maskb = sbuf(nc, ctx, "p3_mask", [128, 128], BF16)
    bqt = sbuf(nc, ctx, "p3_bqt", [128, 8], F32)
    bkt = sbuf(nc, ctx, "p3_bkt", [128, 8], F32)
    ascol = sbuf(nc, ctx, "p3_ascol", [128, 1], F32)
    bvr = sbuf(nc, ctx, "p3_bvr", [1, 1024], F32)
    bvrb = sbuf(nc, ctx, "p3_bvrb", [1, 1024], BF16)
    bor = sbuf(nc, ctx, "p3_bor", [1, 1024], F32)
    borb = sbuf(nc, ctx, "p3_borb", [1, 1024], BF16)
    onesb = sbuf(nc, ctx, "p3_onesb", [1, 128], BF16)

    io_dbg_av_buf = sbuf(nc, ctx, "p3_dbgav", [128, 65], F32) if "dbg_av" in io else None
    ps_s = [psum(nc, ctx, f"p3_pss{i}", [128, 512], F32) for i in range(4)]
    pt_ps = psum(nc, ctx, "p3_ptps", [128, 1024], BF16)
    ps_av = psum(nc, ctx, "p3_psav", [128, 65], F32)
    ps_o = psum(nc, ctx, "p3_pso", [128, 1024], F32)

    ev_can = None
    if canary is not None:
        ev_can = b.v(lambda ve: ve.memset(canary, 7.0), inc=1)
    ev_w = None
    for dst, key in ((wq_s, "wqT1"), (wk_s, "wkT1"), (wv_s, "wvT1"), (wo_s, "woT1")):
        for fic in range(8):
            ev_w = b.dma("g", dst[:, fic * 1024:(fic + 1) * 1024],
                         io[key][fic * 128:(fic + 1) * 128, :],
                         w=[ev_can] if (ev_can is not None and ev_w is None) else ())
    ev_id = b.dma("y", identb, io["c_ident"])
    ev_mk = b.dma("y", maskb, io["c_mask"])
    ev_bq = b.dma("y", bqt, io["bq1t"])
    ev_bk = b.dma("y", bkt, io["bk1t"])
    ev_as = b.dma("y", ascol, io["ascol"])
    ev_bv = b.dma("y", bvr, io["bv1r"])
    ev_bo = b.dma("y", bor, io["bo1r"])
    ev_bvb = b.s(lambda sc: sc.copy(bvrb, bvr), w=[ev_bv], inc=1)
    ev_bob = b.s(lambda sc: sc.copy(borb, bor), w=[ev_bo], inc=1)
    ev_on = b.v(lambda ve: ve.memset(onesb, 1.0), inc=1)
    ev_bqs = b.v(lambda ve: ve.tensor_scalar(bqt, bqt, ascol, None, op0=ALU.mult),
                 w=[ev_bq, ev_as], inc=1)

    prev = {}
    for seq in range(2):
        base = seq * T
        ev_vz = b.v(lambda ve: ve.memset(v_s, 1.0),
                    w=[prev["vs_free"]] if "vs_free" in prev else [], inc=1)
        ev_kv_done = None
        for g in range(4):
            wv_xn = [prev["xnt_free"]] if "xnt_free" in prev else [ev_w]
            ev_xnt = None
            for fic in range(8):
                ev_xnt = b.dma("y", xnt[:, fic * 512:(fic + 1) * 512],
                               io["xn_in"][base + g * 512: base + (g + 1) * 512,
                                           fic * 128:(fic + 1) * 128],
                               w=wv_xn if fic == 0 else (), transpose=True)
            ev_kev = None
            for foc in range(8):
                wv0 = [ev_xnt, ev_bk]
                if g == 0 and "pss_free" in prev:
                    wv0 = wv0 + [prev["pss_free"]]
                if ev_kev is not None:
                    wv0 = [ev_kev]
                ev_mm = None
                for fic in range(8):
                    ev_mm = b.t(lambda te, foc=foc, fic=fic: te.matmul(
                        ps_s[0], wk_s[:, fic * 1024 + foc * 128: fic * 1024 + (foc + 1) * 128],
                        xnt[:, fic * 512:(fic + 1) * 512],
                        start=(fic == 0), stop=(fic == 7)),
                        w=wv0 if fic == 0 else (), inc=(1 if fic == 7 else 0))
                ev_kev = b.s(lambda sc, foc=foc, g=g: sc.activation(
                    k_s[:, foc * 2048 + g * 512: foc * 2048 + (g + 1) * 512], ps_s[0],
                    AF.Identity, bias=bkt[:, foc:foc + 1], scale=1.0), w=[ev_mm], inc=1)
            ev_vev = None
            for tb in range(4):
                gtb = g * 4 + tb
                ev_mmv = None
                wv0 = [ev_xnt, ev_bvb, ev_on, ev_vz]
                if g == 0 and "pss_free" in prev:
                    wv0 = wv0 + [prev["pss_free"]]
                if ev_vev is not None:
                    wv0 = [ev_vev]
                for fic in range(8):
                    for half in range(2):
                        ev_mmv = b.t(lambda te, fic=fic, tb=tb, half=half: te.matmul(
                            ps_s[1 + half],
                            xnt[:, fic * 512 + tb * 128: fic * 512 + (tb + 1) * 128],
                            wv_s[:, fic * 1024 + half * 512: fic * 1024 + (half + 1) * 512],
                            start=(fic == 0), stop=False),
                            w=wv0 if (fic == 0 and half == 0) else ())
                for half in range(2):
                    ev_mmv = b.t(lambda te, half=half: te.matmul(
                        ps_s[1 + half], onesb, bvrb[0:1, half * 512:(half + 1) * 512],
                        start=False, stop=True), inc=(1 if half == 1 else 0))
                ev_vev = None
                for half in range(2):
                    dsth = v_s[:, gtb * 1040 + half * 8 * 65: gtb * 1040 + (half + 1) * 8 * 65]
                    dsth = dsth.rearrange("p (h j) -> p h j", h=8)[:, :, 0:64]
                    ev_vev = b.v(lambda ve, dsth=dsth, half=half: ve.tensor_copy(
                        dsth, ps_s[1 + half].rearrange("p (h j) -> p h j", h=8)),
                        w=[ev_mmv], inc=1)
            ev_kv_done = ev_vev
            prev["xnt_free"] = ev_kv_done

        for qg in range(4):
            wv_xn = [prev["xnt_free"]] if "xnt_free" in prev else []
            ev_xnt = None
            for fic in range(8):
                ev_xnt = b.dma("y", xnt[:, fic * 512:(fic + 1) * 512],
                               io["xn_in"][base + qg * 512: base + (qg + 1) * 512,
                                           fic * 128:(fic + 1) * 128],
                               w=wv_xn if fic == 0 else (), transpose=True)
            ev_qev = None
            for foc in range(8):
                wv0 = [ev_xnt, ev_bqs]
                if "pso_free" in prev:
                    wv0.append(prev.pop("pso_free"))
                if ev_qev is not None:
                    wv0 = [ev_qev]
                ev_mm = None
                for fic in range(8):
                    ev_mm = b.t(lambda te, foc=foc, fic=fic: te.matmul(
                        ps_o[:, (foc % 2) * 512:(foc % 2) * 512 + 512],
                        wq_s[:, fic * 1024 + foc * 128: fic * 1024 + (foc + 1) * 128],
                        xnt[:, fic * 512:(fic + 1) * 512],
                        start=(fic == 0), stop=(fic == 7)),
                        w=wv0 if fic == 0 else (), inc=(1 if fic == 7 else 0))
                wv_q = [ev_mm]
                if "qt_free" in prev and foc == 0:
                    wv_q.append(prev.pop("qt_free"))
                ev_qev = b.s(lambda sc, foc=foc: sc.activation(
                    q_t[:, foc * 512:(foc + 1) * 512],
                    ps_o[:, (foc % 2) * 512:(foc % 2) * 512 + 512],
                    AF.Identity, bias=bqt[:, foc:foc + 1], scale=ascol), w=wv_q, inc=1)
            prev["xnt_free"] = ev_qev
            for qt in range(4):
                q0 = qg * 512 + qt * 128
                L = T - q0
                nkb = L // 128
                nbank = (L + 511) // 512
                ev_attn_done = prev.get("attn_done_ev")
                prev_exp = prev.get("prev_exp_ev")
                for h in range(16):
                    po = (h % 2) * 64
                    foc = h // 2
                    lhs_q = q_t[po:po + 64, foc * 512 + qt * 128: foc * 512 + (qt + 1) * 128]
                    ev_sc = []
                    for j in range(nbank):
                        klen = min(512, L - j * 512)
                        wv0 = []
                        if h == 0 and j == 0:
                            wv0 = [ev_qev, ev_kv_done]
                        if j == 0 and "pss_free" in prev:
                            wv0 = wv0 + [prev["pss_free"]]
                        ev_sc.append(b.t(lambda te, lhs_q=lhs_q, po=po, foc=foc, j=j,
                                         klen=klen, q0=q0: te.matmul(
                            ps_s[j][:, 0:klen], lhs_q,
                            k_s[po:po + 64, foc * 2048 + q0 + j * 512: foc * 2048 + q0 + j * 512 + klen],
                            start=True, stop=True), w=wv0, inc=1))
                    e = None
                    for j in range(nbank):
                        klen = min(512, L - j * 512)
                        wv_r = [ev_sc[j]]
                        if e is not None:
                            wv_r.append(e)
                        elif prev_exp is not None:
                            wv_r.append(prev_exp)
                        e = b.v(lambda ve, j=j, klen=klen: ve.reduce_max(
                            mstat[:, j:j + 1], ps_s[j][:, 0:klen], axis=AX.X, negate=True),
                            w=wv_r, inc=1)
                    wv_c = [e] + ([prev_exp] if prev_exp is not None else [])
                    e = b.v(lambda ve, nbank=nbank: ve.tensor_reduce(
                        mstat[:, 4:5], mstat[:, 0:nbank], axis=AX.X, op=ALU.min), w=wv_c, inc=1)
                    ev_negmax = e
                    ev_exp = None
                    for j in range(nbank):
                        klen = min(512, L - j * 512)
                        ev_exp = b.s(lambda sc, j=j, klen=klen: sc.activation(
                            p_s[:, j * 512: j * 512 + klen], ps_s[j][:, 0:klen],
                            AF.Exp, bias=mstat[:, 4:5], scale=1.0),
                            w=[ev_negmax] if j == 0 else (), inc=(1 if j == nbank - 1 else 0))
                    ev_mask = b.v(lambda ve: ve.tensor_tensor(
                        p_s[:, 0:128], p_s[:, 0:128], maskb, op=ALU.mult),
                        w=[ev_exp, ev_mk], inc=1)
                    ev_pt = None
                    nb8 = (nkb + 7) // 8
                    for bb in range(nb8):
                        n_in = min(8, nkb - bb * 8)
                        ev_tr = None
                        for kk in range(n_in):
                            kb = bb * 8 + kk
                            wv0 = [ev_mask] if (bb == 0 and kk == 0) else ()
                            if kk == 0 and bb > 0:
                                wv0 = [ev_pt]
                            ev_tr = b.t(lambda te, kb=kb, kk=kk: te.transpose(
                                pt_ps[:, kk * 128:(kk + 1) * 128],
                                p_s[:, kb * 128:(kb + 1) * 128], identb),
                                w=wv0, inc=(1 if kk == n_in - 1 else 0))
                        ev_pt = b.s(lambda sc, bb=bb, n_in=n_in: sc.copy(
                            ptb[:, bb * 1024: bb * 1024 + n_in * 128],
                            pt_ps[:, 0:n_in * 128]), w=[ev_tr], inc=1)
                    ev_av = None
                    for kb in range(nkb):
                        ktb = (q0 + kb * 128) // 128
                        wv0 = []
                        if kb == 0:
                            wv0 = [ev_pt, ev_kv_done] + ([prev["psav_free"]] if "psav_free" in prev else [])
                        ev_av = b.t(lambda te, ktb=ktb, kb=kb, h=h: te.matmul(
                            ps_av, ptb[:, kb * 128:(kb + 1) * 128],
                            v_s[:, ktb * 1040 + h * 65: ktb * 1040 + (h + 1) * 65],
                            start=(kb == 0), stop=(kb == nkb - 1)),
                            w=wv0, inc=(1 if kb == nkb - 1 else 0))
                    prev_exp = ev_exp
                    if seq == 0 and qg == 3 and qt == 3 and h == 0 and "dbg_p" in io:
                        ev_dp = b.dma("y", io["dbg_p"], p_s[:, 0:128], w=[ev_mask])
                        b.dma("y", io["dbg_mstat"], mstat, w=[ev_negmax])
                        ev_dav = b.v(lambda ve: ve.tensor_copy(io_dbg_av_buf, ps_av), w=[ev_av], inc=1)
                        b.dma("y", io["dbg_av"], io_dbg_av_buf, w=[ev_dav])
                    wv_z = [ev_av] + ([ev_attn_done] if ev_attn_done is not None else [])
                    e = b.v(lambda ve: ve.tensor_copy(zc[:, 0:1], ps_av[:, 64:65]),
                            w=wv_z, inc=1)
                    e = b.v(lambda ve: ve.reciprocal(zc[:, 1:2], zc[:, 0:1]), w=[e], inc=1)
                    wv_at = [e]
                    if h == 0 and "attn_free" in prev:
                        wv_at.append(prev.pop("attn_free"))
                    ev_attn_done = b.s(lambda sc, h=h: sc.activation(
                        attn_tok[:, h * 64:(h + 1) * 64], ps_av[:, 0:64],
                        AF.Identity, bias=0.0, scale=zc[:, 1:2]), w=wv_at, inc=1)
                    prev["pss_free"] = ev_exp
                    prev["psav_free"] = ev_attn_done
                prev["attn_done_ev"] = ev_attn_done
                prev["prev_exp_ev"] = prev_exp
                r0 = base + q0
                if seq == 0 and qg == 3 and qt == 3 and "dbg_attn" in io:
                    ev_attn_done = b.dma("y", io["dbg_attn"], attn_tok, w=[ev_attn_done])
                ev_ats = b.dma("y", atbuf[r0:r0 + 128, :], attn_tok, w=[ev_attn_done])
                prev["attn_free"] = ev_ats
                ev_att = None
                for fic in range(8):
                    wv0 = [ev_ats] if fic == 0 else ()
                    if fic == 0 and "attnT_free" in prev:
                        wv0 = [ev_ats, prev.pop("attnT_free")]
                    ev_att = b.dma("y", attnT[:, fic * 128:(fic + 1) * 128],
                                   atbuf[r0:r0 + 128, fic * 128:(fic + 1) * 128],
                                   w=wv0, transpose=True)
                ev_wo = None
                wv0 = [ev_att, ev_bob, ev_on, prev.pop("pso_free", None)]
                wv0 = [w for w in wv0 if w is not None]
                for fic in range(8):
                    for half in range(2):
                        ev_wo = b.t(lambda te, fic=fic, half=half: te.matmul(
                            ps_o[:, half * 512:(half + 1) * 512],
                            attnT[:, fic * 128:(fic + 1) * 128],
                            wo_s[:, fic * 1024 + half * 512: fic * 1024 + (half + 1) * 512],
                            start=(fic == 0), stop=False),
                            w=wv0 if (fic == 0 and half == 0) else ())
                for half in range(2):
                    ev_wo = b.t(lambda te, half=half: te.matmul(
                        ps_o[:, half * 512:(half + 1) * 512], onesb,
                        borb[0:1, half * 512:(half + 1) * 512],
                        start=False, stop=True), inc=(1 if half == 1 else 0))
                prev["attnT_free"] = ev_wo
                wv_ob = [ev_wo] + ([prev["obuf_free"]] if "obuf_free" in prev else [])
                ev_ob = b.v(lambda ve: ve.tensor_copy(obuf, ps_o), w=wv_ob, inc=1)
                prev["pso_free"] = ev_ob
                wv_xt = [prev["xt_free"]] if "xt_free" in prev else []
                ev_xt = b.dma("y", xt, io["x_in"][r0:r0 + 128, :], w=wv_xt)
                wv_st = [ev_ob] + ([prev["x3t_free"]] if "x3t_free" in prev else [])
                a_c, b_c, ev_st = ln.stats(obuf, x3t, stat, 0, D, w=wv_st)
                e = b.v(lambda ve: ve.tensor_scalar(x3t, obuf, a_c, b_c,
                                                    op0=ALU.mult, op1=ALU.add), w=[ev_st], inc=1)
                prev["obuf_free"] = e
                ev_x3 = b.v(lambda ve: ve.tensor_tensor(x3t, x3t, xt, op=ALU.add),
                            w=[e, ev_xt], inc=1)
                prev["xt_free"] = ev_x3
                ev_sx = b.dma("y", io["x_out"][r0:r0 + 128, :], x3t, w=[ev_x3])
                prev["x3t_free"] = ev_sx
                wv_st4 = [ev_x3] + ([prev["xn3t_free"]] if "xn3t_free" in prev else [])
                a3, b3c, ev_st4 = ln.stats(x3t, xn3t, stat, 8, D, w=wv_st4)
                ev_ap3 = b.v(lambda ve: ve.tensor_scalar(xn3t, x3t, a3, b3c,
                                                         op0=ALU.mult, op1=ALU.add), w=[ev_st4], inc=1)
                wv_cb3 = [ev_ap3] + ([prev["xn3b_free"]] if "xn3b_free" in prev else [])
                ev_cb3 = b.v(lambda ve: ve.tensor_copy(xn3b, xn3t), w=wv_cb3, inc=1)
                ev_sxn = b.dma("y", io["xn_out"][r0:r0 + 128, :], xn3b, w=[ev_cb3])
                prev["xn3t_free"] = ev_sxn
                prev["xn3b_free"] = ev_sxn
            prev["qt_free"] = ev_attn_done
        prev["vs_free"] = ("t", b.sems["t"].n)
        if seq == 1 and "dbg_wq" in io:
            b.dma("y", io["dbg_wq"], wq_s, w=[prev["vs_free"]])
            b.dma("y", io["dbg_wo"], wo_s, w=[prev["vs_free"]])
        if seq == 1 and "dbg_canary" in io:
            b.dma("y", io["dbg_canary"], canary, w=[prev["vs_free"]])
        if seq == 1 and "dbg_sm" in io:
            b.dma("y", io["dbg_sm"][:, 0:128], identb, w=[prev["vs_free"]])
            b.dma("y", io["dbg_sm"][:, 128:256], maskb)
            b.dma("y", io["dbg_smf"][:, 0:8], bqt, slow=True)
            b.dma("y", io["dbg_smf"][:, 8:16], bkt, slow=True)
            b.dma("y", io["dbg_smf"][:, 16:17], ascol, slow=True)
            b.dma("y", io["dbg_kv"][:, 0:2048], k_s[:, 0:2048])
            b.dma("y", io["dbg_kv"][:, 2048:4096], v_s[:, 0:2048])

    b.wait("y", [("y", b.sems["y"].n)])
    b.wait("g", [("gd", b.sems["gd"].n)])


def build_p3_program():
    nc = bass.Bass("TRN2", target_bir_lowering=False)
    settle_scr = nc.dram_tensor("settle_scr", [8, 128], F32, kind="Internal").ap()
    io = {}
    io["x_in"] = nc.dram_tensor("x_in", [NT, D], F32, kind="ExternalInput").ap()
    io["xn_in"] = nc.dram_tensor("xn_in", [NT, D], BF16, kind="ExternalInput").ap()
    for k in ("wqT1", "wkT1", "wvT1", "woT1"):
        io[k] = nc.dram_tensor(k, [D, D], F32, kind="ExternalInput").ap()
    io["bq1t"] = nc.dram_tensor("bq1t", [128, 8], F32, kind="ExternalInput").ap()
    io["bk1t"] = nc.dram_tensor("bk1t", [128, 8], F32, kind="ExternalInput").ap()
    io["bv1r"] = nc.dram_tensor("bv1r", [1, D], F32, kind="ExternalInput").ap()
    io["bo1r"] = nc.dram_tensor("bo1r", [1, D], F32, kind="ExternalInput").ap()
    io["ascol"] = nc.dram_tensor("ascol", [128, 1], F32, kind="ExternalInput").ap()
    io["c_ident"] = nc.dram_tensor("c_ident", [128, 128], BF16, kind="ExternalInput").ap()
    io["c_mask"] = nc.dram_tensor("c_mask", [128, 128], BF16, kind="ExternalInput").ap()
    io["x_out"] = nc.dram_tensor("x_out", [NT, D], F32, kind="ExternalOutput").ap()
    io["xn_out"] = nc.dram_tensor("xn_out", [NT, D], BF16, kind="ExternalOutput").ap()
    io["atbuf"] = nc.dram_tensor("atbuf", [NT, D], BF16, kind="Internal").ap()
    with ExitStack() as ctx:
        b = B(nc, ctx, sem_offset=16)
        rng = _pre_clean(nc, b, settle_scr=settle_scr)
        emit_p3(b, nc, ctx, io)
        blk = ctx.enter_context(nc.Block())
        b.finalize(blk)
    _post_clean(nc, rng)
    return nc


def build_p2_program():
    nc = bass.Bass("TRN2", target_bir_lowering=False)
    settle_scr = nc.dram_tensor("settle_scr", [8, 128], F32, kind="Internal").ap()
    io = {}
    io["x_in"] = nc.dram_tensor("x_in", [NT, D], F32, kind="ExternalInput").ap()
    io["xn_in"] = nc.dram_tensor("xn_in", [NT, D], BF16, kind="ExternalInput").ap()
    io["w1T"] = nc.dram_tensor("w1T", [D, DFF0], F32, kind="ExternalInput").ap()
    io["b1t"] = nc.dram_tensor("b1t", [128, 32], F32, kind="ExternalInput").ap()
    io["w2T"] = nc.dram_tensor("w2T", [DFF0, D], F32, kind="ExternalInput").ap()
    io["b2r"] = nc.dram_tensor("b2r", [1, D], F32, kind="ExternalInput").ap()
    io["x_out"] = nc.dram_tensor("x_out", [NT, D], F32, kind="ExternalOutput").ap()
    io["xn_out"] = nc.dram_tensor("xn_out", [NT, D], BF16, kind="ExternalOutput").ap()
    with ExitStack() as ctx:
        b = B(nc, ctx, sem_offset=8)
        rng = _pre_clean(nc, b, settle_scr=settle_scr)
        emit_p2(b, nc, ctx, io)
        blk = ctx.enter_context(nc.Block())
        b.finalize(blk)
    _post_clean(nc, rng)
    return nc


def build_p1_program():
    nc = bass.Bass("TRN2", target_bir_lowering=False)
    settle_scr = nc.dram_tensor("settle_scr", [8, 128], F32, kind="Internal").ap()
    io = {}
    io["x_in"] = nc.dram_tensor("x_in", [NT, D], F32, kind="ExternalInput").ap()
    for k in ("w0qT", "w0vT", "w0gT", "w0oT"):
        io[k] = nc.dram_tensor(k, [D, D], F32, kind="ExternalInput").ap()
    io["g0full"] = nc.dram_tensor("g0full", [128, D], F32, kind="ExternalInput").ap()
    io["b0full"] = nc.dram_tensor("b0full", [128, D], F32, kind="ExternalInput").ap()
    io["decay_row"] = nc.dram_tensor("decay_row", [1, 16], F32, kind="ExternalInput").ap()
    io["decay_col"] = nc.dram_tensor("decay_col", [16, 1], F32, kind="ExternalInput").ap()
    io["c_uptri"] = nc.dram_tensor("c_uptri", [128, 128], F32, kind="ExternalInput").ap()
    io["x_out"] = nc.dram_tensor("x_out", [NT, D], F32, kind="ExternalOutput").ap()
    io["xn_out"] = nc.dram_tensor("xn_out", [NT, D], BF16, kind="ExternalOutput").ap()
    io["xn0buf"] = nc.dram_tensor("xn0buf", [NT, D], BF16, kind="Internal").ap()
    io["y2buf"] = nc.dram_tensor("y2buf", [NT, D], BF16, kind="Internal").ap()
    io["r_dram"] = nc.dram_tensor("r_dram", [16, 4096], F32, kind="Internal").ap()
    io["lamp_df"] = nc.dram_tensor("lamp_df", [16, 130], F32, kind="Internal").ap()
    io["lamp_db"] = nc.dram_tensor("lamp_db", [16, 130], BF16, kind="Internal").ap()
    io["l128_d"] = nc.dram_tensor("l128_d", [16, 128], F32, kind="Internal").ap()
    with ExitStack() as ctx:
        b = B(nc, ctx, sem_offset=0)
        rng = _pre_clean(nc, b, settle_scr=settle_scr)
        emit_p1(b, nc, ctx, io)
        blk = ctx.enter_context(nc.Block())
        b.finalize(blk)
    _post_clean(nc, rng)
    return nc



def _pre_clean(nc, b, settle_scr=None):
    nums = sorted(s.h.num for s in b.sems.values())
    junk = nc.alloc_semaphore("junk_settle")
    rng = range(nums[0], max(nums[-1], junk.num) + 1)
    nc.gpsimd.dma_reset(rng)
    nc.gpsimd.sem_clear(rng)
    engs = (nc.sync, nc.scalar, nc.vector, nc.tensor, nc.gpsimd)
    for eng in engs:
        for _ in range(20):
            eng.nop(cycle_cnt=4000)
    if settle_scr is not None:
        n = 0
        for q in (nc.sync, nc.gpsimd):
            for i in range(4):
                q.dma_start(out=settle_scr[i:i + 1, :], in_=settle_scr[4 + i:5 + i, :]
                            ).then_inc(junk, 16)
                n += 16
                q.wait_ge(junk, n)
        nc.gpsimd.dma_reset(rng)
        nc.gpsimd.sem_clear(rng)
        for eng in engs:
            for _ in range(20):
                eng.nop(cycle_cnt=4000)
    return rng


def _post_clean(nc, rng):
    nc.all_engine_barrier()
    nc.gpsimd.dma_reset(rng)
    nc.gpsimd.sem_clear(rng)
    nc.all_engine_barrier()


DELAY_NOPS = 300
DELAY_ENGS = None
DBG_HANDOFF = False


def build_fused_program(upto=4, sem_off=32, frm=1):
    nc = bass.Bass("TRN2", target_bir_lowering=False)
    settle_scr = nc.dram_tensor("settle_scr", [8, 128], F32, kind="Internal").ap()
    io = {}
    io["x0"] = nc.dram_tensor("x0", [NT, D], F32, kind="ExternalInput").ap()
    for k in ("w0qT", "w0vT", "w0gT", "w0oT"):
        io[k] = nc.dram_tensor(k, [D, D], F32, kind="ExternalInput").ap()
    io["g0full"] = nc.dram_tensor("g0full", [128, D], F32, kind="ExternalInput").ap()
    io["b0full"] = nc.dram_tensor("b0full", [128, D], F32, kind="ExternalInput").ap()
    io["decay_row"] = nc.dram_tensor("decay_row", [1, 16], F32, kind="ExternalInput").ap()
    io["decay_col"] = nc.dram_tensor("decay_col", [16, 1], F32, kind="ExternalInput").ap()
    io["c_uptri"] = nc.dram_tensor("c_uptri", [128, 128], F32, kind="ExternalInput").ap()
    io["w1T"] = nc.dram_tensor("w1T", [D, DFF0], F32, kind="ExternalInput").ap()
    io["b1t"] = nc.dram_tensor("b1t", [128, 32], F32, kind="ExternalInput").ap()
    io["w2T"] = nc.dram_tensor("w2T", [DFF0, D], F32, kind="ExternalInput").ap()
    io["b2r"] = nc.dram_tensor("b2r", [1, D], F32, kind="ExternalInput").ap()
    for k in ("wqT1", "wkT1", "wvT1", "woT1"):
        io[k] = nc.dram_tensor(k, [D, D], F32, kind="ExternalInput").ap()
    io["bq1t"] = nc.dram_tensor("bq1t", [128, 8], F32, kind="ExternalInput").ap()
    io["bk1t"] = nc.dram_tensor("bk1t", [128, 8], F32, kind="ExternalInput").ap()
    io["bv1r"] = nc.dram_tensor("bv1r", [1, D], F32, kind="ExternalInput").ap()
    io["bo1r"] = nc.dram_tensor("bo1r", [1, D], F32, kind="ExternalInput").ap()
    io["ascol"] = nc.dram_tensor("ascol", [128, 1], F32, kind="ExternalInput").ap()
    io["c_ident"] = nc.dram_tensor("c_ident", [128, 128], BF16, kind="ExternalInput").ap()
    io["c_mask"] = nc.dram_tensor("c_mask", [128, 128], BF16, kind="ExternalInput").ap()
    io["railmJ"] = nc.dram_tensor("railmJ", [16, 4096], F32, kind="ExternalInput").ap()
    io["railuJ"] = nc.dram_tensor("railuJ", [16, 16384], F32, kind="ExternalInput").ap()
    io["wcT"] = nc.dram_tensor("wcT", [DFF1, D], F32, kind="ExternalInput").ap()
    out_final = nc.dram_tensor("out", [NT, D], F32, kind="ExternalOutput").ap()
    xbuf = [nc.dram_tensor(f"xbuf{i}", [NT, D], F32, kind="Internal").ap() for i in range(3)]
    xnbuf = [nc.dram_tensor(f"xnbuf{i}", [NT, D], BF16, kind="Internal").ap() for i in range(3)]
    io["xn0buf"] = nc.dram_tensor("xn0buf", [NT, D], BF16, kind="Internal").ap()
    io["y2buf"] = nc.dram_tensor("y2buf", [NT, D], BF16, kind="Internal").ap()
    io["r_dram"] = nc.dram_tensor("r_dram", [16, 4096], F32, kind="Internal").ap()
    io["lamp_df"] = nc.dram_tensor("lamp_df", [16, 130], F32, kind="Internal").ap()
    io["lamp_db"] = nc.dram_tensor("lamp_db", [16, 130], BF16, kind="Internal").ap()
    io["l128_d"] = nc.dram_tensor("l128_d", [16, 128], F32, kind="Internal").ap()
    io["atbuf"] = nc.dram_tensor("atbuf", [NT, D], BF16, kind="Internal").ap()
    io["wbd_d"] = nc.dram_tensor("wbd_d", [4096, 512], BF16, kind="Internal").ap()
    io["wcb_d"] = nc.dram_tensor("wcb_d", [DFF1, D], BF16, kind="Internal").ap()

    if frm > 1:
        io["xn_ext"] = nc.dram_tensor("xn_ext", [NT, D], BF16, kind="ExternalInput").ap()
    if DBG_HANDOFF:
        io["dbg_xb1"] = nc.dram_tensor("dbg_xb1", [NT, D], F32, kind="ExternalOutput").ap()
        io["dbg_xnb1"] = nc.dram_tensor("dbg_xnb1", [NT, D], BF16, kind="ExternalOutput").ap()
    flush_d = nc.dram_tensor("flush_d", [128, 128], BF16, kind="Internal").ap()
    with ExitStack() as ctx:
        b = B(nc, ctx, sem_offset=sem_off)
        b.flush_src = flush_d
        b.flush_dst = sbuf(nc, ctx, "flush_sb", [128, 128], BF16)
        rng = _pre_clean(nc, b, settle_scr=settle_scr)
        io1 = dict(io); io1["x_in"] = io["x0"]
        io1["x_out"] = out_final if upto == 1 else xbuf[0]; io1["xn_out"] = xnbuf[0]
        if frm <= 1:
            with ExitStack() as c1:
                emit_p1(b, nc, c1, io1)
                b.bar(delay_nops=DELAY_NOPS)
        if upto >= 2 and frm <= 2:
            io2 = dict(io)
            io2["x_in"] = xbuf[0] if frm <= 1 else io["x0"]
            io2["xn_in"] = xnbuf[0] if frm <= 1 else io["xn_ext"]
            io2["x_out"] = out_final if upto == 2 else xbuf[1]; io2["xn_out"] = xnbuf[1]
            with ExitStack() as c2:
                emit_p2(b, nc, c2, io2)
                b.bar(delay_nops=DELAY_NOPS)
        if upto >= 3 and frm <= 3:
            io3 = dict(io)
            io3["x_in"] = xbuf[1] if frm <= 2 else io["x0"]
            io3["xn_in"] = xnbuf[1] if frm <= 2 else io["xn_ext"]
            io3["x_out"] = out_final if upto == 3 else xbuf[2]; io3["xn_out"] = xnbuf[2]
            with ExitStack() as c3:
                emit_p3(b, nc, c3, io3)
                b.bar(delay_nops=DELAY_NOPS)
        if "dbg_xb1" in io:
            b.wait("y", [("gd", b.sems["gd"].n), ("y", b.sems["y"].n)])
            b.dma("y", io["dbg_xb1"], xbuf[1])
            b.dma("y", io["dbg_xnb1"], xnbuf[1])
        if upto >= 4:
            io4 = dict(io)
            io4["x_in"] = xbuf[2] if frm <= 3 else io["x0"]
            io4["xn_in"] = xnbuf[2] if frm <= 3 else io["xn_ext"]
            io4["x_out"] = out_final
            with ExitStack() as c4:
                emit_p4(b, nc, c4, io4)
                b.bar()
        blk = ctx.enter_context(nc.Block())
        b.finalize(blk)
    _post_clean(nc, rng)
    return nc


_PROG_CACHE = {}


def _get_prog(name, builder):
    if name not in _PROG_CACHE:
        _PROG_CACHE[name] = builder()
    return _PROG_CACHE[name]


_WARMED = set()


def _launch(nc, in_maps, warm=True):
    if warm and id(nc) not in _WARMED:
        run_bass_kernel_spmd(nc, in_maps, core_ids=list(range(NCORE)))
        _WARMED.add(id(nc))
    res = run_bass_kernel_spmd(nc, in_maps, core_ids=list(range(NCORE)))
    return res.results


def _filter(nc, m):
    names = set()
    for alloc in nc.m.functions[0].allocations:
        try:
            if alloc.kind == "ExternalInput":
                names.add(alloc.memorylocations[0].name)
        except AttributeError:
            pass
    return {k: v for k, v in m.items() if k in names}


FUSED = True


def kernel(**inputs):
    staged = stage_inputs(inputs)
    if FUSED:
        pf = _get_prog("fused", build_fused_program)
        maps = []
        for m in staged:
            mm = dict(m)
            mm["x0"] = mm.pop("x_in")
            maps.append(_filter(pf, mm))
        res = _launch(pf, maps, warm=False)
        out = np.concatenate([r["out"] for r in res], axis=0)
        return out.reshape(16, T, D)

    p1 = _get_prog("p1", build_p1_program)
    p2 = _get_prog("p2", build_p2_program)
    p3 = _get_prog("p3", build_p3_program)
    p4 = _get_prog("p4", build_p4_program)

    r1 = _launch(p1, [_filter(p1, m) for m in staged])
    for m, r in zip(staged, r1):
        m["x_in"] = r["x_out"]
        m["xn_in"] = r["xn_out"]
    r2 = _launch(p2, [_filter(p2, m) for m in staged])
    for m, r in zip(staged, r2):
        m["x_in"] = r["x_out"]
        m["xn_in"] = r["xn_out"]
    r3 = _launch(p3, [_filter(p3, m) for m in staged])
    for m, r in zip(staged, r3):
        m["x_in"] = r["x_out"]
        m["xn_in"] = r["xn_out"]
    r4 = _launch(p4, [_filter(p4, m) for m in staged])
    out = np.concatenate([r["x_out"] for r in r4], axis=0)
    return out.reshape(16, T, D)
